# version 12
# baseline (speedup 1.0000x reference)
import math
import contextlib
import numpy as np
import ml_dtypes
import concourse.bass as bass
import concourse.mybir as mybir
from concourse.bass_utils import run_bass_kernel_spmd

F32 = mybir.dt.float32
BF16 = mybir.dt.bfloat16
I32 = mybir.dt.int32
AF = mybir.ActivationFunctionType
ALU = mybir.AluOpType
AX = mybir.AxisListType

D = 2048
INTER = 5632
NIT = 44
NB_GU = NIT // 2
NCT = 16
NH = 16
NKV = 4
HD = 128
KVD = 512
SEQ = 4096
NST = 32
TOWN = 1024
NTT = 8
NKVT = 16
LN_EPS = 1e-5
RMS_EPS = 1e-6
QSCALE = 1.0 / math.sqrt(HD)
EXP_SHIFT = 30.0
MAGIC_F = 123456.0
MAGIC_BITS = int(np.float32(MAGIC_F).view(np.int32))
PI = math.pi
LOG1E4_64 = math.log(10000.0) / 64.0
POLL_BOUND = 500000
PH_LIMIT = 99
CH_COUNT = 4
FFN_SUB = 5


@contextlib.contextmanager
def _clean(cm):
    t = cm.__enter__()
    try:
        yield t
    finally:
        cm.__exit__(None, None, None)


class _PhaseStop(Exception):
    pass


def build_program(evA):
    first = (len(evA) == 0)
    ev = {}
    nc = bass.Bass("TRN2", target_bir_lowering=False)

    def din(name, shape, dt=F32):
        return nc.dram_tensor(name, shape, dt, kind="ExternalInput").ap()

    x_own_in = din("x_own", [TOWN, D])
    x_mir_in = din("x_mir", [TOWN, D])
    wq_in = din("wq", [D, D])
    wk_in = din("wk", [D, KVD])
    wv_in = din("wv", [D, KVD])
    wo_in = din("wo", [D, D])
    gw_in = din("gw", [INTER, D])
    uw_in = din("uw", [INTER, D])
    dw_in = din("dw", [D, INTER])
    gs_in = din("gs_col", [128, NIT])
    us_in = din("us_col", [128, NIT])
    ds_in = din("ds_col", [128, NCT])
    idbf_in = din("ident_bf", [128, 128], BF16)
    idf_in = din("ident_f32", [128, 128], F32)
    ktab_in = din("ktab", [128, NKVT * 2 * 128])

    y_out = nc.dram_tensor("y", [TOWN, D], F32, kind="ExternalOutput").ap()

    def dscr(name, shape, dt):
        return nc.dram_tensor(name, shape, dt).ap()

    gw_bf = dscr("gw_bf", [INTER, D], BF16)
    uw_bf = dscr("uw_bf", [INTER, D], BF16)
    dw_bf = dscr("dw_bf", [D, INTER], BF16)
    wq_bf = dscr("wq_bf", [D, D], BF16)
    wo_bf = dscr("wo_bf", [D, D], BF16)
    wt_g = dscr("wt_g", [NIT, 128, D], BF16)
    wt_u = dscr("wt_u", [NIT, 128, D], BF16)
    wt_d = dscr("wt_d", [NCT, 128, INTER], BF16)
    x1m_scr = dscr("x1m", [NTT, 128, D], BF16)
    xspill = dscr("xspill", [NTT, 128, D], F32)
    k_scr = dscr("k_scr", [NKVT, 128, KVD], BF16)
    v_scr = dscr("v_scr", [NKVT, 128, KVD], BF16)

    xch_k = nc.dram_tensor("xch_k", [2, NKVT, 128, KVD], BF16, addr_space="Shared").ap()
    xch_v = nc.dram_tensor("xch_v", [2, NKVT, 128, KVD], BF16, addr_space="Shared").ap()
    xch_f = nc.dram_tensor("xch_f", [2, 64], F32, addr_space="Shared").ap()

    class Ctr:
        def __init__(self, semh, step):
            self.sem, self.step, self.n = semh, step, 0

        def mark(self, inst, name=None):
            self.n += self.step
            inst.then_inc(self.sem, self.step)
            if name is not None:
                ev[name] = (self.sem, self.n)
            return self.n

    def sbuf_t(*a, **kw):
        return _clean(nc.sbuf_tensor(*a, **kw))

    def psum_t(*a, **kw):
        return _clean(nc.psum_tensor(*a, **kw))

    def W(eng, *names):
        if first:
            return
        for name in names:
            s, v = evA[name]
            eng.wait_ge(s, v)

    with contextlib.ExitStack() as st:
        def sb(name, shape, dt):
            return st.enter_context(sbuf_t(name, shape, dt))

        cG = Ctr(st.enter_context(nc.semaphore("sG")), 16)
        cH = Ctr(st.enter_context(nc.semaphore("sH")), 16)
        cT = Ctr(st.enter_context(nc.semaphore("cT")), 1)
        cV = Ctr(st.enter_context(nc.semaphore("cV")), 1)
        cS = Ctr(st.enter_context(nc.semaphore("cS")), 1)
        cGc = Ctr(st.enter_context(nc.semaphore("cGc")), 1)
        sPoll = st.enter_context(nc.semaphore("sPoll"))

        def mksem(name):
            return Ctr(st.enter_context(nc.semaphore(name)), 16)

        sT = mksem("sTr")
        sCa = [mksem(f"sCa{i}") for i in range(4)]
        sCn = mksem("sCn")
        sStg = [mksem(f"sStg{i}") for i in range(4)]
        sStgA = [mksem(f"sStgA{i}") for i in range(4)]
        sGr = [mksem(f"sGr{i}") for i in range(2)]
        sUr = [mksem(f"sUr{i}") for i in range(2)]
        sDr = [mksem(f"sDr{i}") for i in range(2)]
        sSg = [mksem(f"sSg{i}") for i in range(2)]
        sSu = [mksem(f"sSu{i}") for i in range(2)]
        sSd = [mksem(f"sSd{i}") for i in range(2)]
        sDl = [mksem(f"sDl{i}") for i in range(2)]
        sWk = [mksem(f"sWk{i}") for i in range(2)]
        sWv = [mksem(f"sWv{i}") for i in range(2)]
        sWq = [mksem(f"sWq{i}") for i in range(2)]
        sWo = [mksem(f"sWo{i}") for i in range(2)]
        sX15 = [mksem(f"sX15{i}") for i in range(2)]
        sX2 = [mksem(f"sX2{i}") for i in range(2)]
        sKs = [mksem(f"sKs{i}") for i in range(2)]
        sVs = [mksem(f"sVs{i}") for i in range(2)]
        sXc = [mksem(f"sXc{i}") for i in range(2)]
        sXf = mksem("sXf")
        sVf = [mksem(f"sVf{i}") for i in range(2)]
        sRs = [mksem(f"sRs{i}") for i in range(2)]
        SLOTS = ([sT, sCn, sXf] + sCa + sStg + sStgA + sGr + sUr + sDr + sSg + sSu
                 + sSd + sDl + sWk + sWv + sWq + sWo + sX15 + sX2 + sKs + sVs
                 + sXc + sVf + sRs)
        ALLC = [cG, cH, cT, cV, cS, cGc] + SLOTS

        def sdma(eng, sem_obj, out, in_, name=None, **kw):
            eng.wait_ge(sem_obj.sem, sem_obj.n)
            v = sem_obj.mark(eng.dma_start(out=out, in_=in_, **kw), name)
            return v

        def stpose(eng, sem_obj, out, in_, name=None):
            eng.wait_ge(sem_obj.sem, sem_obj.n)
            v = sem_obj.mark(eng.dma_start_transpose(out, in_), name)
            eng.wait_ge(sem_obj.sem, v)
            return v

        SER_OPS = {"tensor_scalar", "tensor_tensor", "tensor_copy", "reduce_sum",
                   "reduce_max", "reciprocal", "scalar_tensor_tensor",
                   "activation", "iota", "memset"}

        class SerialEng:

            def __init__(self, eng, ctr):
                self._e, self._c = eng, ctr

            def __getattr__(self, n):
                f = getattr(self._e, n)
                if n in SER_OPS:
                    c, e = self._c, self._e

                    def wrap(*a, **k):
                        e.wait_ge(c.sem, c.n)
                        inst = f(*a, **k)
                        c.mark(inst)
                        return inst
                    return wrap
                return f

        class BlockWrap:
            def __init__(self, b):
                self._b = b

            def tensor(self, f):
                return self._b.tensor(f)

            def sync(self, f):
                return self._b.sync(f)

            def vector(self, f):
                return self._b.vector(lambda v: f(SerialEng(v, cV)))

            def scalar(self, f):
                return self._b.scalar(lambda s: f(SerialEng(s, cS)))

            def gpsimd(self, f):
                return self._b.gpsimd(lambda g: f(SerialEng(g, cGc)))

        gs_sb = sb("gs_sb", [128, NIT], F32)
        us_sb = sb("us_sb", [128, NIT], F32)
        ds_sb = sb("ds_sb", [128, NCT], F32)
        id_bf = sb("id_bf", [128, 128], BF16)
        id_f = sb("id_f", [128, 128], F32)
        ones_bf = sb("ones_bf", [128, 1], BF16)
        sq_junk = sb("sq_junk", [128, D], BF16)
        lncol = sb("lncol", [128, 24], F32)
        cbias = sb("cbias", [128, 3], F32)
        flag_sb = sb("flag_sb", [1, 64], F32)
        magic_sb = sb("magic_sb", [1, 64], F32)

        block = BlockWrap(st.enter_context(nc.Block()))

        def barrier(tag, extra=()):
            @block.tensor
            def _(t):
                t.wait_ge(cT.sem, cT.n)
                cT.mark(t.nop(), f"bar{tag}T")

            @block.vector
            def _(v):
                v.wait_ge(cV.sem, cV.n)
                cV.mark(v.nop(), f"bar{tag}V")

            @block.scalar
            def _(s):
                s.wait_ge(cS.sem, cS.n)
                cS.mark(s.nop(), f"bar{tag}S")

            @block.gpsimd
            def _(g):
                W(g, *extra)
                g.wait_ge(cGc.sem, cGc.n)
                cGc.mark(g.nop(), f"bar{tag}G")

            @block.sync
            def _(sy):
                sy.wait_ge(cGc.sem, cGc.n)
                cGc.mark(sy.nop(), f"bar{tag}Y")
            names = [f"bar{tag}{x}" for x in "TVSG"] + [f"bar{tag}Y"]
            for attr in ("tensor", "vector", "scalar", "gpsimd", "sync"):
                getattr(block, attr)(lambda e, _n=tuple(names): W(e, *_n))

        def core_switch(engines, body):
            snap = [c.n for c in ALLC]
            final = None
            for arm in nc.switch_core_id(engines, n=128):
                for c, v in zip(ALLC, snap):
                    c.n = v
                body(int(arm.logical) % 8)
                if final is None:
                    final = [c.n for c in ALLC]
            for c, v in zip(ALLC, final):
                c.n = v

        def cast_gu(g, bi):
            sdma(g, sCa[(2 * bi) % 4], gw_bf[bi * 256:(bi + 1) * 256, :],
                 gw_in[bi * 256:(bi + 1) * 256, :], f"cast_g{bi}")
            sdma(g, sCa[(2 * bi + 1) % 4], uw_bf[bi * 256:(bi + 1) * 256, :],
                 uw_in[bi * 256:(bi + 1) * 256, :], f"cast_u{bi}")

        def cast_dw(g, ht):
            sdma(g, sCa[ht % 4], dw_bf[ht * 128:(ht + 1) * 128, :],
                 dw_in[ht * 128:(ht + 1) * 128, :], f"cast_d{ht}",
                 max_dma_last_dim=8192)

        @block.gpsimd
        def _(g):
            sdma(g, sCn, gs_sb[:, :], gs_in)
            sdma(g, sCn, us_sb[:, :], us_in)
            sdma(g, sCn, ds_sb[:, :], ds_in)
            sdma(g, sCn, id_bf[:, :], idbf_in, "p0_small")
            sdma(g, sCn, id_f[:, :], idf_in, "p0_idf")
            g.memset(ones_bf[:, :], 1.0)
            g.memset(magic_sb[:, :], MAGIC_F)
            g.memset(cbias[:, 0:1], LN_EPS)
            g.memset(cbias[:, 1:2], -PI)
            g.memset(cbias[:, 2:3], -EXP_SHIFT)
            ev["p0_gc"] = (cGc.sem, cGc.n)
            for bi in range(2):
                cast_gu(g, bi)
            for ht in range(2):
                cast_dw(g, ht)

        def emit_bulk_casts():
            @block.gpsimd
            def _(g):
                for bi in range(2, NB_GU):
                    cast_gu(g, bi)
                for ht in range(2, NCT):
                    cast_dw(g, ht)
                for i in range(8):
                    sdma(g, sCa[i % 4], wq_bf[i * 256:(i + 1) * 256, :],
                         wq_in[i * 256:(i + 1) * 256, :], f"cast_wq{i}")
                for i in range(8):
                    sdma(g, sCa[i % 4], wo_bf[i * 256:(i + 1) * 256, :],
                         wo_in[i * 256:(i + 1) * 256, :], f"cast_wo{i}")


        def emit_ln(tag, x_ap, out_ap, cb, pre_out_waits=()):
            n = float(x_ap.shape[-1])

            @block.vector
            def _(v):
                W(v, f"{tag}_src")
                v.reduce_sum(lncol[:, cb:cb + 1], x_ap, axis=AX.X)
                ev[f"{tag}_sum"] = (cV.sem, cV.n)

            @block.scalar
            def _(s):
                W(s, f"{tag}_src")
                s.activation(sq_junk[:, 0:x_ap.shape[-1]], x_ap, AF.Square,
                                     accum_out=lncol[:, cb + 1:cb + 2])
                ev[f"{tag}_ssq"] = (cS.sem, cS.n)

            @block.vector
            def _(v):
                W(v, f"{tag}_ssq")
                v.tensor_scalar(lncol[:, cb:cb + 1], lncol[:, cb:cb + 1],
                                1.0 / n, None, ALU.mult)
                v.tensor_scalar(lncol[:, cb + 1:cb + 2], lncol[:, cb + 1:cb + 2],
                                1.0 / n, None, ALU.mult)
                v.tensor_tensor(lncol[:, cb + 2:cb + 3], lncol[:, cb:cb + 1],
                                lncol[:, cb:cb + 1], ALU.mult)
                v.tensor_tensor(lncol[:, cb + 1:cb + 2],
                                        lncol[:, cb + 1:cb + 2],
                                        lncol[:, cb + 2:cb + 3], ALU.subtract)
                ev[f"{tag}_var"] = (cV.sem, cV.n)

            @block.scalar
            def _(s):
                W(s, f"{tag}_var", "p0_gc")
                s.activation(lncol[:, cb + 2:cb + 3], lncol[:, cb + 1:cb + 2],
                                     AF.Sqrt, bias=cbias[:, 0:1])
                ev[f"{tag}_sd"] = (cS.sem, cS.n)

            @block.vector
            def _(v):
                W(v, f"{tag}_sd", *pre_out_waits)
                v.reciprocal(lncol[:, cb + 3:cb + 4], lncol[:, cb + 2:cb + 3])
                v.tensor_scalar(out_ap, x_ap, lncol[:, cb:cb + 1],
                                        lncol[:, cb + 3:cb + 4],
                                        ALU.subtract, ALU.mult)
                ev[f"{tag}_out"] = (cV.sem, cV.n)

        def emit_ffn_phase(ph, chunks, with_transposes):
            with contextlib.ExitStack() as fst:
                def fsb(name, shape, dt):
                    return fst.enter_context(sbuf_t(ph + name, shape, dt))

                xstage = fsb("xstage", [128, 4, D], F32)
                xn_bf = fsb("xn_bf", [128, 2, D], BF16)
                xnT_ch = fsb("xnT_ch", [128, NCT, 512], BF16)
                hdn = fsb("hdn", [128, NIT, 512], BF16)
                gring = fsb("gring", [128, 2, NCT, 256], BF16)
                uring = fsb("uring", [128, 2, NCT, 256], BF16)
                dring = fsb("dring", [128, 2, NIT, 128], BF16)
                silu_b = fsb("silu_b", [128, 2, 512], BF16)
                dstage = fsb("dstage", [128, 2, INTER], BF16) if with_transposes \
                    else None
                dhat = fsb("dhat", [128, 2, 512], BF16)
                pA = fst.enter_context(psum_t(ph + "pA", [128, 1024], F32))
                pB = fst.enter_context(psum_t(ph + "pB", [128, 1024], F32))
                pC = fst.enter_context(psum_t(ph + "pC", [128, 1024], F32))
                pD1 = fst.enter_context(psum_t(ph + "pD1", [128, 1024], BF16))
                pD2 = fst.enter_context(psum_t(ph + "pD2", [128, 1024], BF16))

                def emit_pipe_block(bi):
                    @block.sync
                    def _(sy):
                        slot = bi % 2
                        if bi >= 2:
                            W(sy, f"gu_mm_done_{2 * (bi - 2) + 1}",
                              f"wtg_st_{bi - 2}", f"wtu_st_{bi - 2}")
                        W(sy, f"cast_g{bi}")
                        stpose(sy, sT, gring[:, slot, :, :],
                               gw_bf[bi * 256:(bi + 1) * 256, :], f"gwT{bi}")
                        for sub in range(2):
                            e = sdma(sy, sSg[slot], wt_g[2 * bi + sub],
                                     gring[:, slot, :, sub * 128:(sub + 1) * 128])
                        ev[f"wtg_st_{bi}"] = (sSg[slot].sem, e)
                        W(sy, f"cast_u{bi}")
                        stpose(sy, sT, uring[:, slot, :, :],
                               uw_bf[bi * 256:(bi + 1) * 256, :], f"uwT{bi}")
                        for sub in range(2):
                            e = sdma(sy, sSu[slot], wt_u[2 * bi + sub],
                                     uring[:, slot, :, sub * 128:(sub + 1) * 128])
                        ev[f"wtu_st_{bi}"] = (sSu[slot].sem, e)

                def emit_pipe_d_load(ht):
                    slot = ht % 2

                    @block.sync
                    def _(sy):
                        if ht >= 2:
                            W(sy, f"dload_used_{ht - 2}")
                        W(sy, f"cast_d{ht}")
                        sdma(sy, sDl[slot], dstage[:, slot, :],
                             dw_bf[ht * 128:(ht + 1) * 128, :], f"dload_{ht}")

                def emit_pipe_d_store(ht):
                    slot = ht % 2

                    @block.sync
                    def _(sy):
                        W(sy, f"dwT{ht}")
                        sdma(sy, sSd[slot], wt_d[ht], dring[:, slot, :, :],
                             f"wtd_st_{ht}")

                def emit_dw_transpose(ht, tag):
                    slot = ht % 2
                    NBATCH = 6

                    @block.tensor
                    def _(t):
                        W(t, f"dload_{ht}", "p0_small")
                        if ht == 0:
                            W(t, f"{tag}_hdn_{NIT - 1}", f"{tag}_hdn_{NIT - 2}")
                        if ht >= 2:
                            W(t, f"wtd_st_{ht - 2}")
                        for b in range(NBATCH):
                            its = range(8 * b, min(8 * b + 8, NIT))
                            pscr = (pA if b % 2 == 0 else pB)[:, 0:512].bitcast(BF16)
                            if b >= 2:
                                W(t, f"dwTev_{ht}_{b - 2}")
                            elif ht >= 1:
                                W(t, f"dwTev_{ht - 1}_{NBATCH - 2 + b}")
                            for k, it in enumerate(its):
                                inst = t.transpose(
                                    pscr[:, k * 128:(k + 1) * 128],
                                    dstage[:, slot, it * 128:(it + 1) * 128],
                                    id_bf[:, :])
                            nm = cT.mark(inst, f"dwT_{ht}_{b}")
                            if b == NBATCH - 1:
                                ev[f"dload_used_{ht}"] = (cT.sem, nm)

                    for b in range(NBATCH):
                        its = range(8 * b, min(8 * b + 8, NIT))
                        nits = len(its)
                        pscr = (pA if b % 2 == 0 else pB)[:, 0:512].bitcast(BF16)

                        @block.vector
                        def _(v, _b=b, _n=nits, _ps=pscr):
                            W(v, f"dwT_{ht}_{_b}")
                            if ht >= 2 and _b == 0 and FFN_SUB >= 4:
                                W(v, f"{tag}_dn_mm_{ht - 2}")
                            v.tensor_copy(
                                dring[:, slot, 8 * _b:8 * _b + _n, :],
                                _ps[:, 0:_n * 128])
                            ev[f"dwTev_{ht}_{_b}"] = (cV.sem, cV.n)
                            nm = cV.n
                            if _b == NBATCH - 1:
                                ev[f"dwT{ht}"] = (cV.sem, nm)

                prev = [None]

                def ffn_chunk(ci, src_kind):
                    tag = f"{ph}{src_kind}{ci}"
                    direct = with_transposes and prev[0] is None
                    ptag = prev[0]

                    if not direct:
                        @block.gpsimd
                        def _(g):
                            if ptag is not None:
                                W(g, f"{ptag}_stored", f"{ptag}_stored0")
                            if src_kind == "o2" and ptag is None:
                                W(g, *[f"x2st_{tt}" for tt in range(NTT)])
                            for tt in range(4):
                                row = ci * 4 + tt
                                if src_kind == "o1":
                                    src = x_own_in[row * 128:(row + 1) * 128, :]
                                elif src_kind == "m":
                                    src = x_mir_in[row * 128:(row + 1) * 128, :]
                                else:
                                    src = xspill[row]
                                sdma(g, sStg[tt], xstage[:, tt, :], src,
                                     f"{tag}_staged{tt}")

                    for tt in range(4):
                        ev[f"{tag}_ln{tt}_src"] = ev[f"{tag}_staged{tt}"]
                        pre = []
                        if tt >= 2:
                            pre.append(f"{tag}_xnT{tt - 2}")
                        elif ptag is not None:
                            pre.append(f"{ptag}_xnT{tt + 2}")
                        emit_ln(f"{tag}_ln{tt}", xstage[:, tt, :], xn_bf[:, tt % 2, :],
                                cb=(tt % 2) * 4, pre_out_waits=tuple(pre))

                        @block.sync
                        def _(sy, _tt=tt):
                            W(sy, f"{tag}_ln{_tt}_out")
                            if _tt == 0 and ptag is not None:
                                W(sy, f"{ptag}_gu_mm_{NIT - 1}")
                            stpose(sy, sT, xnT_ch[:, :, _tt * 128:(_tt + 1) * 128],
                                   xn_bf[:, _tt % 2, :], f"{tag}_xnT{_tt}")

                    if direct and FFN_SUB >= 2:
                        for bi in range(2, NB_GU):
                            emit_pipe_block(bi)
                        if FFN_SUB >= 3:
                            emit_pipe_d_load(0)
                            emit_pipe_d_load(1)

                    if not direct:
                        @block.sync
                        def _(sy):
                            for bi in range(NB_GU):
                                slot = bi % 2
                                if bi >= 2:
                                    W(sy, f"{tag}_gu_mm_{2 * (bi - 2) + 1}")
                                elif ptag is not None:
                                    W(sy, f"{ptag}_gu_mm_{NIT - 4 + 2 * bi + 1}")
                                W(sy, f"wtg_st_{bi}", f"wtu_st_{bi}")
                                if bi < 2:
                                    W(sy, f"wtg_st_{NB_GU - 2 + bi}",
                                      f"wtu_st_{NB_GU - 2 + bi}")
                                for sub in range(2):
                                    sdma(sy, sGr[slot],
                                         gring[:, slot, :, sub * 128:(sub + 1) * 128],
                                         wt_g[2 * bi + sub])
                                    e = sdma(sy, sUr[slot],
                                             uring[:, slot, :, sub * 128:(sub + 1) * 128],
                                             wt_u[2 * bi + sub])
                                ev[f"{tag}_guw{bi}"] = (sUr[slot].sem, e)
                                ev[f"{tag}_guwg{bi}"] = (sGr[slot].sem, sGr[slot].n)
                            for ht in range(NCT):
                                slot = ht % 2
                                if ht >= 2:
                                    W(sy, f"{tag}_dn_mm_{ht - 2}")
                                elif ptag is not None:
                                    W(sy, f"{ptag}_dn_mm_{NCT - 2 + ht}")
                                W(sy, f"wtd_st_{ht}")
                                if ht < 2:
                                    W(sy, f"wtd_st_{NCT - 2 + ht}")
                                sdma(sy, sDr[slot], dring[:, slot, :, :],
                                     wt_d[ht], f"{tag}_dw{ht}")

                    if FFN_SUB < 2:
                        prev[0] = tag
                        return
                    @block.tensor
                    def _(t):
                        for it in range(NIT):
                            bi, sub = it // 2, it % 2
                            slot = bi % 2
                            bank = pA if it % 2 == 0 else pB
                            if sub == 0:
                                if direct:
                                    W(t, f"gwT{bi}", f"uwT{bi}")
                                else:
                                    W(t, f"{tag}_guw{bi}", f"{tag}_guwg{bi}")
                            if it == 0:
                                for tt in range(4):
                                    W(t, f"{tag}_xnT{tt}")
                            if it >= 2:
                                W(t, f"{tag}_hdn_{it - 2}")
                            elif ptag is not None:
                                W(t, f"{ptag}_hdn_{NIT - 2 + it}")
                            for ct in range(NCT):
                                t.matmul(bank[:, 0:512],
                                         gring[:, slot, ct, sub * 128:(sub + 1) * 128],
                                         xnT_ch[:, ct, :],
                                         start=(ct == 0), stop=(ct == NCT - 1))
                            for ct in range(NCT):
                                inst = t.matmul(
                                    bank[:, 512:1024],
                                    uring[:, slot, ct, sub * 128:(sub + 1) * 128],
                                    xnT_ch[:, ct, :],
                                    start=(ct == 0), stop=(ct == NCT - 1))
                            nm = cT.mark(inst, f"{tag}_gu_mm_{it}")
                            if direct:
                                ev[f"gu_mm_done_{it}"] = (cT.sem, nm)

                    for it in range(NIT):
                        bank = pA if it % 2 == 0 else pB

                        @block.scalar
                        def _(s, _it=it, _bank=bank):
                            W(s, f"{tag}_gu_mm_{_it}")
                            if _it == 0 and direct:
                                W(s, "p0_small")
                            if _it < 2 and ptag is not None:
                                W(s, f"{ptag}_hdn_{NIT - 2 + _it}")
                            s.activation(silu_b[:, _it % 2, :], _bank[:, 0:512],
                                                 AF.Silu, scale=gs_sb[:, _it:_it + 1])
                            ev[f"{tag}_silu_{_it}"] = (cS.sem, cS.n)

                        @block.vector
                        def _(v, _it=it, _bank=bank):
                            W(v, f"{tag}_silu_{_it}")
                            if _it == 0 and ptag is not None:
                                W(v, f"{ptag}_dn_mm_{NCT - 1}")
                            v.scalar_tensor_tensor(
                                hdn[:, _it, :], _bank[:, 512:1024],
                                us_sb[:, _it:_it + 1], silu_b[:, _it % 2, :],
                                ALU.mult, ALU.mult)
                            ev[f"{tag}_hdn_{_it}"] = (cV.sem, cV.n)

                    if FFN_SUB < 4 and not (direct and FFN_SUB >= 3):
                        prev[0] = tag
                        return

                    def emit_dev_dtb_res(ht):
                        hp = pC[:, 0:512] if ht % 2 == 0 else pC[:, 512:1024]
                        pD = pD1 if ht % 2 == 0 else pD2

                        @block.vector
                        def _(v, _ht=ht, _hp=hp):
                            W(v, f"{tag}_dn_mm_{_ht}")
                            if _ht == 0 and direct:
                                W(v, "p0_small")
                            if _ht >= 2 and FFN_SUB >= 5:
                                W(v, f"{tag}_dtb_{_ht - 2}")
                            elif ptag is not None:
                                W(v, f"{ptag}_dtb_{NCT - 2 + _ht}")
                            v.tensor_scalar(dhat[:, _ht % 2, :], _hp,
                                                    ds_sb[:, _ht:_ht + 1], 0.5,
                                                    ALU.mult, ALU.mult)
                            ev[f"{tag}_dev_{_ht}"] = (cV.sem, cV.n)

                        if FFN_SUB < 5:
                            return
                        @block.tensor
                        def _(t, _ht=ht, _pD=pD):
                            W(t, f"{tag}_dev_{_ht}")
                            if _ht == 0 and direct:
                                W(t, "p0_small")
                            if _ht >= 2:
                                W(t, f"{tag}_res_{_ht - 2}")
                            elif ptag is not None:
                                W(t, f"{ptag}_res_{NCT - 2 + _ht}")
                            for sub in range(4):
                                inst = t.transpose(
                                    _pD[:, sub * 256:sub * 256 + 128],
                                    dhat[:, _ht % 2, sub * 128:(sub + 1) * 128],
                                    id_bf[:, :])
                            cT.mark(inst, f"{tag}_dtb_{_ht}")

                        @block.vector
                        def _(v, _ht=ht, _pD=pD):
                            W(v, f"{tag}_dtb_{_ht}")
                            for sub in range(4):
                                inst = v.tensor_tensor(
                                    xstage[:, sub, _ht * 128:(_ht + 1) * 128],
                                    xstage[:, sub, _ht * 128:(_ht + 1) * 128],
                                    _pD[:, sub * 256:sub * 256 + 128], ALU.add)
                            inst
                            ev[f"{tag}_res_{_ht}"] = (cV.sem, cV.n)

                    if direct:
                        for ht in range(NCT):
                            emit_dw_transpose(ht, tag)
                            emit_pipe_d_store(ht)
                            if ht + 2 < NCT:
                                emit_pipe_d_load(ht + 2)
                            if FFN_SUB < 4:
                                continue

                            @block.tensor
                            def _(t, _ht=ht):
                                slot = _ht % 2
                                hp = pC[:, 0:512] if _ht % 2 == 0 else pC[:, 512:1024]
                                W(t, f"dwT{_ht}")
                                if _ht >= 2:
                                    W(t, f"{tag}_dev_{_ht - 2}")
                                for it in range(NIT):
                                    inst = t.matmul(hp, dring[:, slot, it, :],
                                                    hdn[:, it, :],
                                                    start=(it == 0),
                                                    stop=(it == NIT - 1))
                                nm = cT.mark(inst, f"{tag}_dn_mm_{_ht}")
                                ev[f"dn_mm_done0_{_ht}"] = (cT.sem, nm)
                            emit_dev_dtb_res(ht)
                    if FFN_SUB < 4:
                        prev[0] = tag
                        return
                    if not direct:
                        for ht in range(NCT):
                            @block.tensor
                            def _(t, _ht=ht):
                                slot = _ht % 2
                                hp = (pC[:, 0:512] if _ht % 2 == 0
                                      else pC[:, 512:1024])
                                if _ht == 0:
                                    W(t, f"{tag}_hdn_{NIT - 1}",
                                      f"{tag}_hdn_{NIT - 2}")
                                W(t, f"{tag}_dw{_ht}")
                                if _ht >= 2:
                                    W(t, f"{tag}_dev_{_ht - 2}")
                                elif ptag is not None:
                                    W(t, f"{ptag}_dev_{NCT - 2 + _ht}")
                                for it in range(NIT):
                                    inst = t.matmul(hp, dring[:, slot, it, :],
                                                    hdn[:, it, :],
                                                    start=(it == 0),
                                                    stop=(it == NIT - 1))
                                cT.mark(inst, f"{tag}_dn_mm_{_ht}")
                            emit_dev_dtb_res(ht)

                    if FFN_SUB < 5:
                        prev[0] = tag
                        return
                    @block.gpsimd
                    def _(g):
                        W(g, *[f"{tag}_res_{ht}" for ht in range(NCT)])
                        for tt in range(4):
                            row = ci * 4 + tt
                            if src_kind == "o1":
                                sdma(g, sRs[tt % 2], xspill[row], xstage[:, tt, :])
                            elif src_kind == "m":
                                sdma(g, sRs[tt % 2], x1m_scr[row], xstage[:, tt, :])
                            else:
                                sdma(g, sRs[tt % 2],
                                     y_out[row * 128:(row + 1) * 128, :],
                                     xstage[:, tt, :])
                        ev[f"{tag}_stored"] = (sRs[1].sem, sRs[1].n)
                        ev[f"{tag}_stored0"] = (sRs[0].sem, sRs[0].n)

                    prev[0] = tag

                if with_transposes:
                    @block.gpsimd
                    def _(g):
                        for tt in range(4):
                            sdma(g, sStg[tt], xstage[:, tt, :],
                                 x_own_in[tt * 128:(tt + 1) * 128, :],
                                 f"{ph}o10_staged{tt}")
                    emit_bulk_casts()
                    emit_pipe_block(0)
                    emit_pipe_block(1)
                for (ci, kind) in chunks:
                    ffn_chunk(ci, kind)
                return prev[0]

        try:
            if PH_LIMIT < 1:
                raise _PhaseStop()
            last1 = emit_ffn_phase("f1",
                                   [(0, "o1"), (1, "o1"), (0, "m"), (1, "m")][:CH_COUNT],
                                   with_transposes=True)
            barrier("p1", extra=((f"{last1}_stored", f"{last1}_stored0")
                                 if FFN_SUB >= 5 else ()))
            if PH_LIMIT < 2:
                raise _PhaseStop()

            with contextlib.ExitStack() as pR:
                def rsb(name, shape, dt):
                    return pR.enter_context(sbuf_t("R" + name, shape, dt,
                                                           side="right"))

                qT = rsb("qT", [128, NCT, TOWN], BF16)
                encT = rsb("encT", [128, NH, TOWN], BF16)
                r_cols = rsb("rcols", [128, NTT, NH], F32)

                with contextlib.ExitStack() as p2:
                    def psb(name, shape, dt):
                        return p2.enter_context(sbuf_t("p2" + name, shape, dt))

                    ktab = psb("ktab", [128, NKVT, 2, 128], F32)
                    xn2_bf = psb("xn2", [128, 2, D], BF16)
                    xsf = psb("xsf", [128, 2, D], F32)
                    xsb = psb("xsb", [128, 2, D], BF16)
                    xnTw = psb("xnTw", [128, NCT, 256], BF16)
                    wkb = psb("wkb", [128, NCT, KVD], BF16)
                    wvb = psb("wvb", [128, NCT, KVD], BF16)
                    khat = psb("khat", [128, 2, KVD], F32)
                    ropet = psb("ropet", [128, 2, KVD], F32)
                    krot = psb("krot", [128, 2, KVD], BF16)
                    vrot = psb("vrot", [128, 2, KVD], BF16)

                    with contextlib.ExitStack() as skv:
                        pK = skv.enter_context(psum_t("pK", [128, 512], F32))
                        pV = skv.enter_context(psum_t("pVp", [128, 512], F32))
                        pK2 = skv.enter_context(psum_t("pK2", [128, 512], F32))
                        pV2 = skv.enter_context(psum_t("pV2", [128, 512], F32))

                        @block.gpsimd
                        def _(g):
                            sdma(g, sCn, ktab[:, :, :, :], ktab_in, "ktab_ld")
                            for ct in range(NCT):
                                sdma(g, sCa[ct % 4], wkb[:, ct, :],
                                     wk_in[ct * 128:(ct + 1) * 128, :])
                                e = sdma(g, sCa[ct % 4], wvb[:, ct, :],
                                         wv_in[ct * 128:(ct + 1) * 128, :])
                                ev[f"wkvb_{ct}"] = (sCa[ct % 4].sem, e)
                        for kt in range(NKVT):
                            ev[f"ktab{kt}_done"] = ev["ktab_ld"]

                        pair_list = [(8, 9), (10, 11), (12, 13), (14, 15),
                                     (0, 1), (2, 3), (4, 5), (6, 7)]
                        for pi, (lt0, lt1) in enumerate(pair_list):
                            mir = lt0 >= 8
                            ptag = f"kvp{pi}"
                            stage = xsb if mir else xsf

                            @block.gpsimd
                            def _(g, _pi=pi, _lt0=lt0, _mir=mir, _st=stage, _ptag=ptag):
                                if _pi >= 1:
                                    W(g, f"kvp{_pi - 1}_ln1_out")
                                for j in range(2):
                                    row = (_lt0 % 8) + j
                                    srcd = x1m_scr[row] if _mir else xspill[row]
                                    e = sdma(g, cG, _st[:, j, :], srcd)
                                ev[f"{_ptag}_staged"] = (cG.sem, e)

                            for j in range(2):
                                ev[f"{ptag}_ln{j}_src"] = ev[f"{ptag}_staged"]
                                pre = [f"kvp{pi - 1}_xnT{j}"] if pi >= 1 else []
                                emit_ln(f"{ptag}_ln{j}", stage[:, j, :], xn2_bf[:, j, :],
                                        cb=j * 4, pre_out_waits=tuple(pre))

                                @block.sync
                                def _(sy, _j=j, _ptag=ptag, _pi=pi):
                                    W(sy, f"{_ptag}_ln{_j}_out")
                                    if _pi >= 1:
                                        W(sy, f"kvp{_pi - 1}_vmm1")
                                    e = cH.mark(sy.dma_start_transpose(
                                        xnTw[:, :, _j * 128:(_j + 1) * 128],
                                        xn2_bf[:, _j, :]), f"{_ptag}_xnT{_j}")
                                    sy.wait_ge(cH.sem, e)

                            @block.tensor
                            def _(t, _pi=pi, _ptag=ptag):
                                W(t, f"{_ptag}_xnT0", f"{_ptag}_xnT1")
                                if _pi == 0:
                                    W(t, *[f"wkvb_{ct}" for ct in range(NCT)])
                                if _pi >= 1:
                                    W(t, f"kvp{_pi - 1}_khat0", f"kvp{_pi - 1}_khat1",
                                      f"kvp{_pi - 1}_vev0", f"kvp{_pi - 1}_vev1")
                                for ct in range(NCT):
                                    st_, sp_ = (ct == 0), (ct == NCT - 1)
                                    t.matmul(pK[:, :], xnTw[:, ct, 0:128],
                                             wkb[:, ct, :], start=st_, stop=sp_)
                                    t.matmul(pK2[:, :], xnTw[:, ct, 128:256],
                                             wkb[:, ct, :], start=st_, stop=sp_)
                                    t.matmul(pV[:, :], xnTw[:, ct, 0:128],
                                             wvb[:, ct, :], start=st_, stop=sp_)
                                    inst = t.matmul(pV2[:, :], xnTw[:, ct, 128:256],
                                                    wvb[:, ct, :], start=st_, stop=sp_)
                                cT.mark(inst, f"{_ptag}_mm_c{NCT - 1}")
                                ev[f"{_ptag}_vmm1"] = (cT.sem, cT.n)

                            for j, lt in enumerate((lt0, lt1)):
                                tg = f"{ptag}k{j}"
                                pKx = pK if j == 0 else pK2
                                pVx = pV if j == 0 else pV2
                                cb = 8 + j * 4

                                @block.scalar
                                def _(s, _tg=tg, _pK=pKx, _cb=cb, _ptag=ptag):
                                    W(s, f"{_ptag}_mm_c{NCT - 1}")
                                    s.activation(sq_junk[:, 0:KVD], _pK[:, :],
                                                         AF.Square,
                                                         accum_out=lncol[:, _cb:_cb + 1])
                                    ev[f"{_tg}_ssq"] = (cS.sem, cS.n)

                                @block.vector
                                def _(v, _tg=tg, _cb=cb):
                                    W(v, f"{_tg}_ssq")
                                    v.tensor_scalar(lncol[:, _cb + 1:_cb + 2],
                                                            lncol[:, _cb:_cb + 1],
                                                            1.0 / KVD, RMS_EPS,
                                                            ALU.mult, ALU.add)
                                    ev[f"{_tg}_ms"] = (cV.sem, cV.n)

                                @block.scalar
                                def _(s, _tg=tg, _cb=cb):
                                    W(s, f"{_tg}_ms")
                                    s.activation(lncol[:, _cb + 2:_cb + 3],
                                                         lncol[:, _cb + 1:_cb + 2],
                                                         AF.Sqrt)
                                    ev[f"{_tg}_sd"] = (cS.sem, cS.n)

                                @block.vector
                                def _(v, _tg=tg, _cb=cb, _lt=lt, _j=j, _pK=pKx,
                                      _pi=pi, _ptag=ptag):
                                    W(v, f"{_tg}_sd", f"ktab{_lt}_done")
                                    if _pi >= 1:
                                        W(v, f"kvp{_pi - 1}_kst{_j}")
                                    v.reciprocal(lncol[:, _cb + 3:_cb + 4],
                                                 lncol[:, _cb + 2:_cb + 3])
                                    v.tensor_scalar(khat[:, _j, :], _pK[:, :],
                                                    lncol[:, _cb + 3:_cb + 4], None,
                                                    ALU.mult)
                                    kh = khat[:, _j, :].rearrange("p (n h) -> p n h", n=NKV)
                                    rt = ropet[:, _j, :].rearrange("p (n h) -> p n h", n=NKV)
                                    kr = krot[:, _j, :].rearrange("p (n h) -> p n h", n=NKV)
                                    cosb = ktab[:, _lt, 1, :].unsqueeze(1).broadcast_to(
                                        [128, NKV, 128])
                                    sinlo = ktab[:, _lt, 0, 0:64].unsqueeze(1).broadcast_to(
                                        [128, NKV, 64])
                                    sinhi = ktab[:, _lt, 0, 64:128].unsqueeze(1).broadcast_to(
                                        [128, NKV, 64])
                                    v.tensor_tensor(rt, kh, cosb, ALU.mult)
                                    v.tensor_tensor(kr[:, :, 0:64], kh[:, :, 64:128],
                                                    sinlo, ALU.mult)
                                    v.tensor_tensor(kr[:, :, 64:128], kh[:, :, 0:64],
                                                    sinhi, ALU.mult)
                                    v.tensor_tensor(krot[:, _j, :],
                                                            ropet[:, _j, :],
                                                            krot[:, _j, :], ALU.add)
                                    ev[f"{_ptag}_khat{_j}"] = (cV.sem, cV.n)

                                @block.vector
                                def _(v, _tg=tg, _j=j, _pV=pVx, _pi=pi, _ptag=ptag):
                                    W(v, f"{_ptag}_mm_c{NCT - 1}")
                                    if _pi >= 1:
                                        W(v, f"kvp{_pi - 1}_vst{_j}")
                                    v.tensor_copy(vrot[:, _j, :], _pV[:, :])
                                    ev[f"{_ptag}_vev{_j}"] = (cV.sem, cV.n)

                                @block.gpsimd
                                def _(g, _ptag=ptag, _j=j, _lt=lt):
                                    W(g, f"{_ptag}_khat{_j}")
                                    sdma(g, cG, k_scr[_lt], krot[:, _j, :],
                                         f"{_ptag}_kst{_j}")
                                    W(g, f"{_ptag}_vev{_j}")
                                    sdma(g, cG, v_scr[_lt], vrot[:, _j, :],
                                         f"{_ptag}_vst{_j}")

                        def xw_body(rank):
                            m = rank % 2
                            g = nc.gpsimd
                            W(g, "kvp7_kst0", "kvp7_kst1", "kvp7_vst0", "kvp7_vst1")
                            for lt in range(NKVT):
                                sdma(g, sXc[lt % 2], xch_k[m, lt], k_scr[lt])
                                sdma(g, sXc[lt % 2], xch_v[m, lt], v_scr[lt])
                            g.wait_ge(sXc[0].sem, sXc[0].n)
                            g.wait_ge(sXc[1].sem, sXc[1].n)
                            sdma(g, sXf, xch_f[m, :], magic_sb[:, :], "flagw")

                        @block.gpsimd
                        def _(g):
                            core_switch([nc.gpsimd], xw_body)

                    with contextlib.ExitStack() as sq:
                        def qsb(name, shape, dt):
                            return sq.enter_context(sbuf_t("q" + name, shape, dt))

                        qhat = qsb("qhat", [128, D], F32)
                        qtmp = qsb("qtmp", [128, D], F32)
                        qrot = qsb("qrot", [128, 2, D], BF16)
                        wq_ring = qsb("wqr", [128, 2, D], BF16)
                        pQ1 = sq.enter_context(psum_t("pQ1", [128, 2048], F32))
                        pQ2 = sq.enter_context(psum_t("pQ2", [128, 2048], F32))

                        for ps in range(4):
                            @block.gpsimd
                            def _(g, _ps=ps):
                                if _ps == 0:
                                    W(g, "flagw")
                                    W(g, "kvp7_ln0_out", "kvp7_ln1_out")
                                else:
                                    W(g, f"qp{_ps - 1}_ln1_out")
                                for j in range(2):
                                    e = sdma(g, cG, xsf[:, j, :],
                                             xspill[2 * _ps + j])
                                ev[f"qp{_ps}_staged"] = (cG.sem, e)

                            for j in range(2):
                                tgl = f"qp{ps}_ln{j}"
                                ev[f"{tgl}_src"] = ev[f"qp{ps}_staged"]
                                if ps == 0:
                                    pre = [f"kvp7_xnT{j}"]
                                else:
                                    pre = [f"qp{ps - 1}_xnT{j}"]
                                emit_ln(tgl, xsf[:, j, :], xn2_bf[:, j, :], cb=j * 4,
                                        pre_out_waits=tuple(pre))

                                @block.sync
                                def _(sy, _j=j, _ps=ps):
                                    W(sy, f"qp{_ps}_ln{_j}_out")
                                    if _ps >= 1:
                                        W(sy, f"q_mm_p{_ps - 1}c{NCT - 1}")
                                    elif _j == 0:
                                        W(sy, "kvp7_vmm1")
                                    e = cH.mark(sy.dma_start_transpose(
                                        xnTw[:, :, _j * 128:(_j + 1) * 128],
                                        xn2_bf[:, _j, :]), f"qp{_ps}_xnT{_j}")
                                    sy.wait_ge(cH.sem, e)

                            @block.gpsimd
                            def _(g, _ps=ps):
                                for ct in range(NCT):
                                    slot = ct % 2
                                    if not (_ps == 0 and ct < 2):
                                        pct = ct - 2 if ct >= 2 else NCT - 2 + ct
                                        ppi = _ps if ct >= 2 else _ps - 1
                                        W(g, f"q_mm_p{ppi}c{pct}")
                                    if _ps == 0:
                                        W(g, f"cast_wq{ct // 2}")
                                    e = sdma(g, cG, wq_ring[:, slot, :],
                                             wq_bf[ct * 128:(ct + 1) * 128, :])
                                    ev[f"wq_p{_ps}c{ct}"] = (cG.sem, e)

                            @block.tensor
                            def _(t, _ps=ps):
                                for ct in range(NCT):
                                    slot = ct % 2
                                    W(t, f"wq_p{_ps}c{ct}")
                                    if ct == 0:
                                        W(t, f"qp{_ps}_xnT0", f"qp{_ps}_xnT1")
                                        if _ps >= 1:
                                            W(t, f"q{2 * _ps - 2}_rope",
                                              f"q{2 * _ps - 1}_rope")
                                    for j in range(2):
                                        pQ = pQ1 if j == 0 else pQ2
                                        for nb in range(4):
                                            inst = t.matmul(
                                                pQ[:, nb * 512:(nb + 1) * 512],
                                                xnTw[:, ct, j * 128:(j + 1) * 128],
                                                wq_ring[:, slot, nb * 512:(nb + 1) * 512],
                                                start=(ct == 0), stop=(ct == NCT - 1))
                                        if j == 1 and ct == NCT - 1:
                                            cT.mark(inst, f"q_mm_p{_ps}c{ct}")
                                            ev[f"q_mm_tt{2 * _ps + 1}"] = (cT.sem, cT.n)
                                        elif j == 1:
                                            cT.mark(inst, f"q_mm_p{_ps}c{ct}")
                                        elif ct == NCT - 1:
                                            cT.mark(inst, f"q_mm_tt{2 * _ps}")

                            for j in range(2):
                                tt = 2 * ps + j
                                pQ = pQ1 if j == 0 else pQ2
                                cb = 16 + j * 4
                                tg = f"q{tt}"

                                @block.scalar
                                def _(s, _tg=tg, _pQ=pQ, _cb=cb, _tt=tt):
                                    W(s, f"q_mm_tt{_tt}")
                                    s.activation(sq_junk[:, :], _pQ[:, :],
                                                         AF.Square,
                                                         accum_out=lncol[:, _cb:_cb + 1])
                                    ev[f"{_tg}_ssq"] = (cS.sem, cS.n)

                                @block.vector
                                def _(v, _tg=tg, _cb=cb):
                                    W(v, f"{_tg}_ssq")
                                    v.tensor_scalar(lncol[:, _cb + 1:_cb + 2],
                                                            lncol[:, _cb:_cb + 1],
                                                            1.0 / D, RMS_EPS,
                                                            ALU.mult, ALU.add)
                                    ev[f"{_tg}_ms"] = (cV.sem, cV.n)

                                @block.scalar
                                def _(s, _tg=tg, _cb=cb):
                                    W(s, f"{_tg}_ms")
                                    s.activation(lncol[:, _cb + 2:_cb + 3],
                                                         lncol[:, _cb + 1:_cb + 2],
                                                         AF.Sqrt)
                                    ev[f"{_tg}_sd"] = (cS.sem, cS.n)

                                @block.vector
                                def _(v, _tg=tg, _cb=cb, _tt=tt, _pQ=pQ, _j=j):
                                    W(v, f"{_tg}_sd", f"ktab{_tt}_done")
                                    if _tt >= 2:
                                        W(v, f"qT_{_tt - 2}")
                                    v.reciprocal(lncol[:, _cb + 3:_cb + 4],
                                                 lncol[:, _cb + 2:_cb + 3])
                                    v.tensor_scalar(lncol[:, _cb + 3:_cb + 4],
                                                    lncol[:, _cb + 3:_cb + 4], QSCALE,
                                                    None, ALU.mult)
                                    v.tensor_scalar(qhat[:, :], _pQ[:, :],
                                                    lncol[:, _cb + 3:_cb + 4], None,
                                                    ALU.mult)
                                    qh = qhat[:, :].rearrange("p (n h) -> p n h", n=NH)
                                    tm = qtmp[:, :].rearrange("p (n h) -> p n h", n=NH)
                                    qr = qrot[:, _j, :].rearrange("p (n h) -> p n h", n=NH)
                                    cosb = ktab[:, _tt, 1, :].unsqueeze(1).broadcast_to(
                                        [128, NH, 128])
                                    sinlo = ktab[:, _tt, 0, 0:64].unsqueeze(1).broadcast_to(
                                        [128, NH, 64])
                                    sinhi = ktab[:, _tt, 0, 64:128].unsqueeze(1).broadcast_to(
                                        [128, NH, 64])
                                    v.tensor_tensor(tm, qh, cosb, ALU.mult)
                                    v.tensor_tensor(qr[:, :, 0:64], qh[:, :, 64:128],
                                                    sinlo, ALU.mult)
                                    v.tensor_tensor(qr[:, :, 64:128], qh[:, :, 0:64],
                                                    sinhi, ALU.mult)
                                    v.tensor_tensor(qrot[:, _j, :], qtmp[:, :],
                                                            qrot[:, _j, :], ALU.add)
                                    ev[f"{_tg}_rope"] = (cV.sem, cV.n)
                                    ev[f"q{_tt}_rope"] = ev[f"{_tg}_rope"]

                                @block.sync
                                def _(sy, _tg=tg, _tt=tt, _j=j):
                                    W(sy, f"{_tg}_rope")
                                    e = cH.mark(sy.dma_start_transpose(
                                        qT[:, :, _tt * 128:(_tt + 1) * 128],
                                        qrot[:, _j, :]), f"qT_{_tt}")
                                    sy.wait_ge(cH.sem, e)

                barrier("p2")
                if PH_LIMIT < 3:
                    raise _PhaseStop()

                with contextlib.ExitStack() as p4s:
                    def asb(name, shape, dt):
                        return p4s.enter_context(sbuf_t("a" + name, shape, dt))

                    kT_sb = asb("kT", [128, NKV, SEQ], BF16)
                    v_full = asb("vf", [128, NST, KVD], BF16)

                    P3REGS = []

                    def p3_body(rank):
                        m = rank % 2
                        q = rank % 4
                        qm = (q + 2) % 4
                        q2 = (rank ^ 1) % 4
                        qm2 = (q2 + 2) % 4
                        sy = nc.sync
                        W(sy, "flagw")
                        r_go, r_c, r_f, r_t, r_w = P3REGS
                        sy.reg_mov(r_go, 1)
                        sy.reg_mov(r_c, 0)
                        sy.reg_mov(r_w, 0)
                        with sy.While(r_go):
                            sy.dma_start(out=flag_sb[:, :],
                                         in_=xch_f[1 - m, :]).then_inc(sPoll, 16)
                            sy.reg_alu(r_w, r_w, 16, ALU.add)
                            sy.wait_ge(sPoll, r_w)
                            sy.load(r_f, flag_sb[0:1, 0:1].bitcast(I32))
                            sy.reg_alu(r_f, r_f, MAGIC_BITS, ALU.bitwise_xor)
                            sy.reg_alu(r_c, r_c, 1, ALU.add)
                            sy.reg_alu(r_t, r_c, POLL_BOUND, ALU.is_lt)
                            sy.reg_alu(r_go, r_f, r_t, ALU.logical_and)
                        for gt in range(NST):
                            quad = gt // 8
                            if quad == q:
                                src_k, src_v = k_scr[gt % 8], v_scr[gt % 8]
                            elif quad == qm:
                                src_k, src_v = k_scr[8 + gt % 8], v_scr[8 + gt % 8]
                            elif quad == q2:
                                src_k, src_v = xch_k[1 - m, gt % 8], xch_v[1 - m, gt % 8]
                            else:
                                src_k = xch_k[1 - m, 8 + gt % 8]
                                src_v = xch_v[1 - m, 8 + gt % 8]
                            e = cH.mark(sy.dma_start_transpose(
                                kT_sb[:, :, gt * 128:(gt + 1) * 128], src_k))
                            sy.wait_ge(cH.sem, e)
                            sdma(sy, cH, v_full[:, gt, :], src_v)
                        sy.wait_ge(cH.sem, cH.n)

                    @block.sync
                    def _(sy):
                        with sy.register("r_go") as r_go, sy.register("r_c") as r_c, \
                             sy.register("r_f") as r_f, sy.register("r_t") as r_t, \
                             sy.register("r_w") as r_w:
                            P3REGS.extend([r_go, r_c, r_f, r_t, r_w])
                            core_switch([nc.sync], p3_body)
                        sy.wait_ge(cGc.sem, cGc.n)
                        cGc.mark(sy.nop(), "p3_done")
                    if PH_LIMIT < 4:
                        raise _PhaseStop()

                    with contextlib.ExitStack() as p4c:
                        pT_sb = p4c.enter_context(
                            sbuf_t("pT", [128, 2, 1024], BF16))
                        S_row = p4c.enter_context(sbuf_t("Srow", [1, TOWN], F32))
                        r_row = p4c.enter_context(sbuf_t("rrow", [1, TOWN], F32))
                        pLA = p4c.enter_context(psum_t("pLA", [128, 1024], F32))
                        pLB = p4c.enter_context(psum_t("pLB", [128, 1024], F32))
                        pE = p4c.enter_context(psum_t("pE", [128, 1024], F32))
                        pS = p4c.enter_context(psum_t("pS", [128, 1024], F32))

                        for n in range(NH):
                            kv = n // 4
                            tgh = f"h{n}"

                            @block.tensor
                            def _(t, _n=n, _kv=kv, _tg=tgh):
                                if _n == 0:
                                    W(t, "p3_done", "p0_gc")
                                    for tt2 in range(NTT):
                                        W(t, f"qT_{tt2}")
                                for stp in range(NST + 2):
                                    if stp < NST:
                                        bank = pLA if stp % 2 == 0 else pLB
                                        if stp >= 2:
                                            W(t, f"{_tg}_exp_{stp - 2}")
                                        elif _n > 0:
                                            W(t, f"h{_n - 1}_exp_{NST - 2 + stp}")
                                        for nb in range(2):
                                            inst = t.matmul(
                                                bank[:, nb * 512:(nb + 1) * 512],
                                                kT_sb[:, _kv, stp * 128:(stp + 1) * 128],
                                                qT[:, _n, nb * 512:(nb + 1) * 512],
                                                start=True, stop=True)
                                        cT.mark(inst, f"{_tg}_qk_{stp}")
                                    su = stp - 2
                                    if su >= 0:
                                        W(t, f"{_tg}_exp_{su}")
                                        if su == 0 and _n > 0:
                                            W(t, f"h{_n - 1}_eev", f"h{_n - 1}_rcol")
                                        for nb in range(2):
                                            t.matmul(pE[:, nb * 512:(nb + 1) * 512],
                                                     v_full[:, su, _kv * 128:(_kv + 1) * 128],
                                                     pT_sb[:, su % 2, nb * 512:(nb + 1) * 512],
                                                     start=(su == 0), stop=(su == NST - 1))
                                        for nb in range(2):
                                            inst = t.matmul(
                                                pS[0:1, nb * 512:(nb + 1) * 512],
                                                ones_bf[:, :],
                                                pT_sb[:, su % 2, nb * 512:(nb + 1) * 512],
                                                start=(su == 0), stop=(su == NST - 1))
                                        cT.mark(inst, f"{_tg}_pv_{su}")

                            for stp in range(NST):
                                @block.scalar
                                def _(s, _n=n, _stp=stp, _tg=tgh):
                                    W(s, f"{_tg}_qk_{_stp}")
                                    if _stp >= 2:
                                        W(s, f"{_tg}_pv_{_stp - 2}")
                                    elif _n > 0:
                                        W(s, f"h{_n - 1}_pv_{NST - 2 + _stp}")
                                    bank = pLA if _stp % 2 == 0 else pLB
                                    s.activation(pT_sb[:, _stp % 2, :], bank[:, :],
                                                         AF.Exp, bias=cbias[:, 2:3])
                                    ev[f"{_tg}_exp_{_stp}"] = (cS.sem, cS.n)

                            @block.vector
                            def _(v, _n=n, _tg=tgh):
                                W(v, f"{_tg}_pv_{NST - 1}")
                                v.tensor_copy(encT[:, _n, :], pE[:, :])
                                ev[f"{_tg}_eev"] = (cV.sem, cV.n)
                                v.tensor_copy(S_row[:, :], pS[0:1, :])
                                v.reciprocal(r_row[:, :], S_row[:, :])
                                ev[f"{_tg}_sev"] = (cV.sem, cV.n)

                            @block.tensor
                            def _(t, _n=n, _tg=tgh):
                                W(t, f"{_tg}_eev", f"{_tg}_sev", "p0_idf")
                                for qt in range(NTT):
                                    inst = t.transpose(pE[:, qt:qt + 1],
                                                       r_row[0:1, qt * 128:(qt + 1) * 128],
                                                       id_f[0:1, 0:1])
                                cT.mark(inst, f"{_tg}_rT")

                            @block.vector
                            def _(v, _n=n, _tg=tgh):
                                W(v, f"{_tg}_rT")
                                v.tensor_copy(r_cols[:, :, _n], pE[:, 0:NTT])
                                ev[f"{_tg}_rcol"] = (cV.sem, cV.n)

                        barrier("p4")
                        if PH_LIMIT < 5:
                            raise _PhaseStop()

                    with contextlib.ExitStack() as p5s:
                        enc_tok = p5s.enter_context(
                            sbuf_t("etok", [128, NTT, D], BF16))
                        wo_ring = p5s.enter_context(
                            sbuf_t("wor", [128, 4, D], BF16))
                        x15 = p5s.enter_context(
                            sbuf_t("x15", [128, 2, D], F32))
                        pW1 = p5s.enter_context(psum_t("pW1", [128, 2048], F32))
                        pW2 = p5s.enter_context(psum_t("pW2", [128, 2048], F32))

                        for n in range(NH):
                            @block.sync
                            def _(sy, _n=n):
                                W(sy, f"h{_n}_eev")
                                e = cH.mark(sy.dma_start_transpose(
                                    enc_tok[:, :, _n * 128:(_n + 1) * 128],
                                    encT[:, _n, :]), f"etok_{_n}")
                                sy.wait_ge(cH.sem, e)

                        for qt in range(NTT):
                            @block.vector
                            def _(v, _qt=qt):
                                W(v, *[f"etok_{n}" for n in range(NH)])
                                W(v, *[f"h{n}_rcol" for n in range(NH)])
                                for n in range(NH):
                                    inst = v.tensor_scalar(
                                        enc_tok[:, _qt, n * 128:(n + 1) * 128],
                                        enc_tok[:, _qt, n * 128:(n + 1) * 128],
                                        r_cols[:, _qt, n:n + 1], None, ALU.mult)
                                inst
                                ev[f"enorm_{_qt}"] = (cV.sem, cV.n)

                        encTn = encT
                        for qt in range(NTT):
                            @block.sync
                            def _(sy, _qt=qt):
                                W(sy, f"enorm_{_qt}")
                                if _qt == 0:
                                    W(sy, *[f"etok_{n}" for n in range(NH)])
                                e = cH.mark(sy.dma_start_transpose(
                                    encTn[:, :, _qt * 128:(_qt + 1) * 128],
                                    enc_tok[:, _qt, :]), f"encTn_{_qt}")
                                sy.wait_ge(cH.sem, e)

                        @block.gpsimd
                        def _(g):
                            for tt in range(2):
                                e = sdma(g, cG, x15[:, tt % 2, :], xspill[tt])
                                ev[f"x1r_{tt}"] = (cG.sem, e)

                        def emit_wo_loads(ps):
                            @block.gpsimd
                            def _(g):
                                for ct in range(NCT):
                                    slot = ct % 4
                                    if not (ps == 0 and ct < 4):
                                        pct = ct - 4 if ct >= 4 else NCT - 4 + ct
                                        pps = ps if ct >= 4 else ps - 1
                                        W(g, f"wo_mm_p{pps}c{pct}")
                                    if ps == 0:
                                        W(g, f"cast_wo{ct // 2}")
                                    e = sdma(g, cG, wo_ring[:, slot, :],
                                             wo_bf[ct * 128:(ct + 1) * 128, :])
                                    ev[f"wo_p{ps}c{ct}"] = (cG.sem, e)

                        emit_wo_loads(0)
                        emit_wo_loads(1)

                        @block.tensor
                        def _(t):
                            for ps in range(4):
                                for ct in range(NCT):
                                    slot = ct % 4
                                    W(t, f"wo_p{ps}c{ct}")
                                    if ct == 0:
                                        for qt in range(NTT):
                                            W(t, f"encTn_{qt}")
                                        if ps >= 1:
                                            W(t, f"attnres_{2 * ps - 2}",
                                              f"attnres_{2 * ps - 1}")
                                    for j in range(2):
                                        tt = 2 * ps + j
                                        pW = pW1 if j == 0 else pW2
                                        for nb in range(4):
                                            inst = t.matmul(
                                                pW[:, nb * 512:(nb + 1) * 512],
                                                encTn[:, ct, tt * 128:(tt + 1) * 128],
                                                wo_ring[:, slot, nb * 512:(nb + 1) * 512],
                                                start=(ct == 0), stop=(ct == NCT - 1))
                                        if j == 1 and ct == NCT - 1:
                                            cT.mark(inst, f"wo_mm_p{ps}c{ct}")
                                            ev[f"wo_mm_tt{2 * ps + 1}"] = (cT.sem, cT.n)
                                        elif j == 1:
                                            cT.mark(inst, f"wo_mm_p{ps}c{ct}")
                                        elif ct == NCT - 1:
                                            cT.mark(inst, f"wo_mm_tt{2 * ps}")

                        def emit_p5_store(tt):
                            pW = pW1 if tt % 2 == 0 else pW2

                            @block.vector
                            def _(v, _tt=tt, _pW=pW):
                                W(v, f"wo_mm_tt{_tt}", f"x1r_{_tt}")
                                v.tensor_tensor(x15[:, _tt % 2, :],
                                                        x15[:, _tt % 2, :],
                                                        _pW[:, :], ALU.add)
                                ev[f"attnres_{_tt}"] = (cV.sem, cV.n)

                            @block.gpsimd
                            def _(g, _tt=tt):
                                W(g, f"attnres_{_tt}")
                                sdma(g, sX2[_tt % 2], xspill[_tt],
                                     x15[:, _tt % 2, :], f"x2st_{_tt}")
                                if _tt + 2 < NTT:
                                    W(g, f"x2st_{_tt}")
                                    sdma(g, sX15[_tt % 2], x15[:, _tt % 2, :],
                                         xspill[_tt + 2], f"x1r_{_tt + 2}")

                        emit_p5_store(0)
                        emit_p5_store(1)
                        emit_wo_loads(2)
                        emit_p5_store(2)
                        emit_p5_store(3)
                        emit_wo_loads(3)
                        for tt in range(4, NTT):
                            emit_p5_store(tt)

            barrier("p5", extra=tuple(f"x2st_{tt}" for tt in range(NTT)))
            if PH_LIMIT < 6:
                raise _PhaseStop()

            emit_ffn_phase("f2", [(0, "o2"), (1, "o2")], with_transposes=False)

        except _PhaseStop:
            pass

        @block.sync
        def _(sy):
            sy.wait_ge(cG.sem, cG.n)
            sy.wait_ge(cH.sem, cH.n)
            sy.wait_ge(cT.sem, cT.n)
            sy.wait_ge(cV.sem, cV.n)
            sy.wait_ge(cS.sem, cS.n)
            sy.wait_ge(cGc.sem, cGc.n)

    return nc, ev


def _events_table(ev):
    return {k: v for k, v in ev.items()}


def _build():
    nc1, ev1 = build_program({})
    nc2, ev2 = build_program(ev1)
    assert set(ev1) == set(ev2), (set(ev1) ^ set(ev2))
    return nc2


_NC_CACHE = [None]


def _mk_ktab(pos):
    inv_ts = 10000.0 ** (-np.arange(64, dtype=np.float64) / 64.0)
    ang = pos.astype(np.float64)[:, None] * inv_ts[None, :]
    sn, cs = np.sin(ang), np.cos(ang)
    tab = np.empty((NKVT, 128, 2, 128), np.float64)
    tab[:, :, 0, 0:64] = -sn.reshape(NKVT, 128, 64)
    tab[:, :, 0, 64:128] = sn.reshape(NKVT, 128, 64)
    tab[:, :, 1, 0:64] = cs.reshape(NKVT, 128, 64)
    tab[:, :, 1, 64:128] = cs.reshape(NKVT, 128, 64)
    return np.ascontiguousarray(
        tab.transpose(1, 0, 2, 3).reshape(128, NKVT * 2 * 128)).astype(np.float32)


def kernel(**inputs):
    x = np.asarray(inputs["L0_attn_x"])
    gate_s = np.asarray(inputs["L0_ffn_gate_s"])
    up_s = np.asarray(inputs["L0_ffn_up_s"])
    down_s = np.asarray(inputs["L0_ffn_down_s"])

    if _NC_CACHE[0] is None:
        _NC_CACHE[0] = _build()
    nc = _NC_CACHE[0]

    gs_col = np.ascontiguousarray(gate_s.reshape(NIT, 128).T).astype(np.float32)
    us_col = np.ascontiguousarray(up_s.reshape(NIT, 128).T).astype(np.float32)
    ds_col = np.ascontiguousarray(down_s.reshape(NCT, 128).T).astype(np.float32)

    shared = {
        "wq": np.ascontiguousarray(inputs["L0_attn_Wq"], np.float32),
        "wk": np.ascontiguousarray(inputs["L0_attn_Wk"], np.float32),
        "wv": np.ascontiguousarray(inputs["L0_attn_Wv"], np.float32),
        "wo": np.ascontiguousarray(inputs["L0_attn_Wout"], np.float32),
        "gw": np.ascontiguousarray(inputs["L0_ffn_gate_w"], np.float32),
        "uw": np.ascontiguousarray(inputs["L0_ffn_up_w"], np.float32),
        "dw": np.ascontiguousarray(inputs["L0_ffn_down_w"], np.float32),
        "gs_col": gs_col, "us_col": us_col, "ds_col": ds_col,
        "ident_bf": np.eye(128, dtype=ml_dtypes.bfloat16),
        "ident_f32": np.eye(128, dtype=np.float32),
    }
    in_maps = []
    for r in range(8):
        b, q = r // 4, r % 4
        qm = (q + 2) % 4
        pos = np.concatenate([np.arange(q * TOWN, (q + 1) * TOWN),
                              np.arange(qm * TOWN, (qm + 1) * TOWN)])
        m = dict(shared)
        m["x_own"] = np.ascontiguousarray(x[b, q * TOWN:(q + 1) * TOWN, :], np.float32)
        m["x_mir"] = np.ascontiguousarray(x[b, qm * TOWN:(qm + 1) * TOWN, :], np.float32)
        m["ktab"] = _mk_ktab(pos)
        in_maps.append(m)

    res = run_bass_kernel_spmd(nc, in_maps, core_ids=list(range(8)))
    out = np.empty((2, SEQ, D), np.float32)
    for r in range(8):
        b, q = r // 4, r % 4
        out[b, q * TOWN:(q + 1) * TOWN, :] = res.results[r]["y"]
    return out


# version 13
# speedup vs baseline: 1.1083x; 1.1083x over previous
import math
import contextlib
import numpy as np
import ml_dtypes
import concourse.bass as bass
import concourse.mybir as mybir
from concourse.bass_utils import run_bass_kernel_spmd

F32 = mybir.dt.float32
BF16 = mybir.dt.bfloat16
I32 = mybir.dt.int32
AF = mybir.ActivationFunctionType
ALU = mybir.AluOpType
AX = mybir.AxisListType

D = 2048
INTER = 5632
NIT = 44
NB_GU = NIT // 2
NCT = 16
NH = 16
NKV = 4
HD = 128
KVD = 512
SEQ = 4096
NST = 32
TOWN = 1024
NTT = 8
NKVT = 16
LN_EPS = 1e-5
RMS_EPS = 1e-6
QSCALE = 1.0 / math.sqrt(HD)
EXP_SHIFT = 30.0
MAGIC_F = 123456.0
MAGIC_BITS = int(np.float32(MAGIC_F).view(np.int32))
PI = math.pi
LOG1E4_64 = math.log(10000.0) / 64.0
POLL_BOUND = 500000
PH_LIMIT = 99
CH_COUNT = 4
FFN_SUB = 5


@contextlib.contextmanager
def _clean(cm):
    t = cm.__enter__()
    try:
        yield t
    finally:
        cm.__exit__(None, None, None)


class _PhaseStop(Exception):
    pass


def build_program(evA):
    first = (len(evA) == 0)
    ev = {}
    nc = bass.Bass("TRN2", target_bir_lowering=False)

    def din(name, shape, dt=F32):
        return nc.dram_tensor(name, shape, dt, kind="ExternalInput").ap()

    x_own_in = din("x_own", [TOWN, D])
    x_mir_in = din("x_mir", [TOWN, D])
    wq_in = din("wq", [D, D])
    wk_in = din("wk", [D, KVD])
    wv_in = din("wv", [D, KVD])
    wo_in = din("wo", [D, D])
    gw_in = din("gw", [INTER, D])
    uw_in = din("uw", [INTER, D])
    dw_in = din("dw", [D, INTER])
    gs_in = din("gs_col", [128, NIT])
    us_in = din("us_col", [128, NIT])
    ds_in = din("ds_col", [128, NCT])
    idbf_in = din("ident_bf", [128, 128], BF16)
    idf_in = din("ident_f32", [128, 128], F32)
    ktab_in = din("ktab", [128, NKVT * 2 * 128])

    y_out = nc.dram_tensor("y", [TOWN, D], F32, kind="ExternalOutput").ap()

    def dscr(name, shape, dt):
        return nc.dram_tensor(name, shape, dt).ap()

    gw_bf = dscr("gw_bf", [INTER, D], BF16)
    uw_bf = dscr("uw_bf", [INTER, D], BF16)
    dw_bf = dscr("dw_bf", [D, INTER], BF16)
    wq_bf = dscr("wq_bf", [D, D], BF16)
    wo_bf = dscr("wo_bf", [D, D], BF16)
    wt_g = dscr("wt_g", [NIT, 128, D], BF16)
    wt_u = dscr("wt_u", [NIT, 128, D], BF16)
    wt_d = dscr("wt_d", [NCT, 128, INTER], BF16)
    x1m_scr = dscr("x1m", [NTT, 128, D], BF16)
    xspill = dscr("xspill", [NTT, 128, D], F32)
    k_scr = dscr("k_scr", [NKVT, 128, KVD], BF16)
    v_scr = dscr("v_scr", [NKVT, 128, KVD], BF16)

    xch_k = nc.dram_tensor("xch_k", [2, NKVT, 128, KVD], BF16, addr_space="Shared").ap()
    xch_v = nc.dram_tensor("xch_v", [2, NKVT, 128, KVD], BF16, addr_space="Shared").ap()
    xch_f = nc.dram_tensor("xch_f", [2, 64], F32, addr_space="Shared").ap()

    class Ctr:
        def __init__(self, semh, step):
            self.sem, self.step, self.n = semh, step, 0

        def mark(self, inst, name=None):
            self.n += self.step
            inst.then_inc(self.sem, self.step)
            if name is not None:
                ev[name] = (self.sem, self.n)
            return self.n

    def sbuf_t(*a, **kw):
        return _clean(nc.sbuf_tensor(*a, **kw))

    def psum_t(*a, **kw):
        return _clean(nc.psum_tensor(*a, **kw))

    def W(eng, *names):
        if first:
            return
        for name in names:
            s, v = evA[name]
            eng.wait_ge(s, v)

    with contextlib.ExitStack() as st:
        def sb(name, shape, dt):
            return st.enter_context(sbuf_t(name, shape, dt))

        cG = Ctr(st.enter_context(nc.semaphore("sG")), 16)
        cH = Ctr(st.enter_context(nc.semaphore("sH")), 16)
        cT = Ctr(st.enter_context(nc.semaphore("cT")), 1)
        cV = Ctr(st.enter_context(nc.semaphore("cV")), 1)
        cS = Ctr(st.enter_context(nc.semaphore("cS")), 1)
        cGc = Ctr(st.enter_context(nc.semaphore("cGc")), 1)
        sPoll = st.enter_context(nc.semaphore("sPoll"))

        def mksem(name):
            return Ctr(st.enter_context(nc.semaphore(name)), 16)

        sT = mksem("sTr")
        sCa = [mksem(f"sCa{i}") for i in range(4)]
        sCn = mksem("sCn")
        sStg = [mksem(f"sStg{i}") for i in range(4)]
        sStgA = [mksem(f"sStgA{i}") for i in range(4)]
        sGr = [mksem(f"sGr{i}") for i in range(2)]
        sUr = [mksem(f"sUr{i}") for i in range(2)]
        sDr = [mksem(f"sDr{i}") for i in range(2)]
        sSg = [mksem(f"sSg{i}") for i in range(2)]
        sSu = [mksem(f"sSu{i}") for i in range(2)]
        sSd = [mksem(f"sSd{i}") for i in range(2)]
        sDl = [mksem(f"sDl{i}") for i in range(2)]
        sWk = [mksem(f"sWk{i}") for i in range(2)]
        sWv = [mksem(f"sWv{i}") for i in range(2)]
        sWq = [mksem(f"sWq{i}") for i in range(2)]
        sWo = [mksem(f"sWo{i}") for i in range(2)]
        sX15 = [mksem(f"sX15{i}") for i in range(2)]
        sX2 = [mksem(f"sX2{i}") for i in range(2)]
        sKs = [mksem(f"sKs{i}") for i in range(2)]
        sVs = [mksem(f"sVs{i}") for i in range(2)]
        sXc = [mksem(f"sXc{i}") for i in range(2)]
        sXf = mksem("sXf")
        sVf = [mksem(f"sVf{i}") for i in range(2)]
        sRs = [mksem(f"sRs{i}") for i in range(2)]
        SLOTS = ([sT, sCn, sXf] + sCa + sStg + sStgA + sGr + sUr + sDr + sSg + sSu
                 + sSd + sDl + sWk + sWv + sWq + sWo + sX15 + sX2 + sKs + sVs
                 + sXc + sVf + sRs)
        ALLC = [cG, cH, cT, cV, cS, cGc] + SLOTS

        def sdma(eng, sem_obj, out, in_, name=None, **kw):
            eng.wait_ge(sem_obj.sem, sem_obj.n)
            v = sem_obj.mark(eng.dma_start(out=out, in_=in_, **kw), name)
            return v

        def stpose(eng, sem_obj, out, in_, name=None):
            eng.wait_ge(sem_obj.sem, sem_obj.n)
            v = sem_obj.mark(eng.dma_start_transpose(out, in_), name)
            eng.wait_ge(sem_obj.sem, v)
            return v

        SER_OPS = {"tensor_scalar", "tensor_tensor", "tensor_copy", "reduce_sum",
                   "reduce_max", "reciprocal", "scalar_tensor_tensor",
                   "activation", "iota", "memset"}

        class SerialEng:

            def __init__(self, eng, ctr):
                self._e, self._c = eng, ctr

            def __getattr__(self, n):
                f = getattr(self._e, n)
                if n in SER_OPS:
                    c, e = self._c, self._e

                    def wrap(*a, **k):
                        e.wait_ge(c.sem, c.n)
                        inst = f(*a, **k)
                        c.mark(inst)
                        return inst
                    return wrap
                return f

        class BlockWrap:
            def __init__(self, b):
                self._b = b

            def tensor(self, f):
                return self._b.tensor(f)

            def sync(self, f):
                return self._b.sync(f)

            def vector(self, f):
                return self._b.vector(lambda v: f(SerialEng(v, cV)))

            def scalar(self, f):
                return self._b.scalar(lambda s: f(SerialEng(s, cS)))

            def gpsimd(self, f):
                return self._b.gpsimd(lambda g: f(SerialEng(g, cGc)))

        gs_sb = sb("gs_sb", [128, NIT], F32)
        us_sb = sb("us_sb", [128, NIT], F32)
        ds_sb = sb("ds_sb", [128, NCT], F32)
        id_bf = sb("id_bf", [128, 128], BF16)
        id_f = sb("id_f", [128, 128], F32)
        ones_bf = sb("ones_bf", [128, 1], BF16)
        sq_junk = sb("sq_junk", [128, D], BF16)
        lncol = sb("lncol", [128, 24], F32)
        cbias = sb("cbias", [128, 3], F32)
        flag_sb = sb("flag_sb", [1, 64], F32)
        magic_sb = sb("magic_sb", [1, 64], F32)

        block = BlockWrap(st.enter_context(nc.Block()))

        def barrier(tag, extra=()):
            @block.tensor
            def _(t):
                t.wait_ge(cT.sem, cT.n)
                cT.mark(t.nop(), f"bar{tag}T")

            @block.vector
            def _(v):
                v.wait_ge(cV.sem, cV.n)
                cV.mark(v.nop(), f"bar{tag}V")

            @block.scalar
            def _(s):
                s.wait_ge(cS.sem, cS.n)
                cS.mark(s.nop(), f"bar{tag}S")

            @block.gpsimd
            def _(g):
                W(g, *extra)
                g.wait_ge(cGc.sem, cGc.n)
                cGc.mark(g.nop(), f"bar{tag}G")

            @block.sync
            def _(sy):
                sy.wait_ge(cGc.sem, cGc.n)
                cGc.mark(sy.nop(), f"bar{tag}Y")
            names = [f"bar{tag}{x}" for x in "TVSG"] + [f"bar{tag}Y"]
            for attr in ("tensor", "vector", "scalar", "gpsimd", "sync"):
                getattr(block, attr)(lambda e, _n=tuple(names): W(e, *_n))

        def core_switch(engines, body):
            snap = [c.n for c in ALLC]
            final = None
            for arm in nc.switch_core_id(engines, n=128):
                for c, v in zip(ALLC, snap):
                    c.n = v
                body(int(arm.logical) % 8)
                if final is None:
                    final = [c.n for c in ALLC]
            for c, v in zip(ALLC, final):
                c.n = v

        def cast_gu(g, bi):
            sdma(g, sCa[(2 * bi) % 4], gw_bf[bi * 256:(bi + 1) * 256, :],
                 gw_in[bi * 256:(bi + 1) * 256, :], f"cast_g{bi}")
            sdma(g, sCa[(2 * bi + 1) % 4], uw_bf[bi * 256:(bi + 1) * 256, :],
                 uw_in[bi * 256:(bi + 1) * 256, :], f"cast_u{bi}")

        def cast_dw(g, ht):
            sdma(g, sCa[ht % 4], dw_bf[ht * 128:(ht + 1) * 128, :],
                 dw_in[ht * 128:(ht + 1) * 128, :], f"cast_d{ht}",
                 max_dma_last_dim=8192)

        @block.gpsimd
        def _(g):
            sdma(g, sCn, gs_sb[:, :], gs_in)
            sdma(g, sCn, us_sb[:, :], us_in)
            sdma(g, sCn, ds_sb[:, :], ds_in)
            sdma(g, sCn, id_bf[:, :], idbf_in, "p0_small")
            sdma(g, sCn, id_f[:, :], idf_in, "p0_idf")
            g.memset(ones_bf[:, :], 1.0)
            g.memset(magic_sb[:, :], MAGIC_F)
            g.memset(cbias[:, 0:1], LN_EPS)
            g.memset(cbias[:, 1:2], -PI)
            g.memset(cbias[:, 2:3], -EXP_SHIFT)
            ev["p0_gc"] = (cGc.sem, cGc.n)
            for bi in range(2):
                cast_gu(g, bi)
            for ht in range(2):
                cast_dw(g, ht)

        def emit_bulk_casts():
            @block.gpsimd
            def _(g):
                for bi in range(2, NB_GU):
                    cast_gu(g, bi)
                for ht in range(2, NCT):
                    cast_dw(g, ht)
                for i in range(8):
                    sdma(g, sCa[i % 4], wq_bf[i * 256:(i + 1) * 256, :],
                         wq_in[i * 256:(i + 1) * 256, :], f"cast_wq{i}")
                for i in range(8):
                    sdma(g, sCa[i % 4], wo_bf[i * 256:(i + 1) * 256, :],
                         wo_in[i * 256:(i + 1) * 256, :], f"cast_wo{i}")


        def emit_ln(tag, x_ap, out_ap, cb, pre_out_waits=()):
            n = float(x_ap.shape[-1])

            @block.vector
            def _(v):
                W(v, f"{tag}_src")
                v.reduce_sum(lncol[:, cb:cb + 1], x_ap, axis=AX.X)
                ev[f"{tag}_sum"] = (cV.sem, cV.n)

            @block.scalar
            def _(s):
                W(s, f"{tag}_src")
                s.activation(sq_junk[:, 0:x_ap.shape[-1]], x_ap, AF.Square,
                                     accum_out=lncol[:, cb + 1:cb + 2])
                ev[f"{tag}_ssq"] = (cS.sem, cS.n)

            @block.vector
            def _(v):
                W(v, f"{tag}_ssq")
                v.tensor_scalar(lncol[:, cb:cb + 1], lncol[:, cb:cb + 1],
                                1.0 / n, None, ALU.mult)
                v.tensor_scalar(lncol[:, cb + 1:cb + 2], lncol[:, cb + 1:cb + 2],
                                1.0 / n, None, ALU.mult)
                v.tensor_tensor(lncol[:, cb + 2:cb + 3], lncol[:, cb:cb + 1],
                                lncol[:, cb:cb + 1], ALU.mult)
                v.tensor_tensor(lncol[:, cb + 1:cb + 2],
                                        lncol[:, cb + 1:cb + 2],
                                        lncol[:, cb + 2:cb + 3], ALU.subtract)
                ev[f"{tag}_var"] = (cV.sem, cV.n)

            @block.scalar
            def _(s):
                W(s, f"{tag}_var", "p0_gc")
                s.activation(lncol[:, cb + 2:cb + 3], lncol[:, cb + 1:cb + 2],
                                     AF.Sqrt, bias=cbias[:, 0:1])
                ev[f"{tag}_sd"] = (cS.sem, cS.n)

            @block.vector
            def _(v):
                W(v, f"{tag}_sd", *pre_out_waits)
                v.reciprocal(lncol[:, cb + 3:cb + 4], lncol[:, cb + 2:cb + 3])
                v.tensor_scalar(out_ap, x_ap, lncol[:, cb:cb + 1],
                                        lncol[:, cb + 3:cb + 4],
                                        ALU.subtract, ALU.mult)
                ev[f"{tag}_out"] = (cV.sem, cV.n)

        def emit_ffn_phase(ph, chunks, with_transposes):
            with contextlib.ExitStack() as fst:
                def fsb(name, shape, dt):
                    return fst.enter_context(sbuf_t(ph + name, shape, dt))

                xstage = fsb("xstage", [128, 4, D], F32)
                xn_bf = fsb("xn_bf", [128, 2, D], BF16)
                xnT_ch = fsb("xnT_ch", [128, NCT, 512], BF16)
                hdn = fsb("hdn", [128, NIT, 512], BF16)
                gring = fsb("gring", [128, 2, NCT, 256], BF16)
                uring = fsb("uring", [128, 2, NCT, 256], BF16)
                dring = fsb("dring", [128, 2, NIT, 128], BF16)
                silu_b = fsb("silu_b", [128, 2, 512], BF16)
                dstage = fsb("dstage", [128, 2, INTER], BF16) if with_transposes \
                    else None
                dhat = fsb("dhat", [128, 2, 512], BF16)
                pA = fst.enter_context(psum_t(ph + "pA", [128, 1024], F32))
                pB = fst.enter_context(psum_t(ph + "pB", [128, 1024], F32))
                pC = fst.enter_context(psum_t(ph + "pC", [128, 1024], F32))
                pD1 = fst.enter_context(psum_t(ph + "pD1", [128, 1024], BF16))
                pD2 = fst.enter_context(psum_t(ph + "pD2", [128, 1024], BF16))

                def emit_pipe_block(bi):
                    @block.sync
                    def _(sy):
                        slot = bi % 2
                        if bi >= 2:
                            W(sy, f"gu_mm_done_{2 * (bi - 2) + 1}",
                              f"wtg_st_{bi - 2}", f"wtu_st_{bi - 2}")
                        W(sy, f"cast_g{bi}")
                        stpose(sy, sT, gring[:, slot, :, :],
                               gw_bf[bi * 256:(bi + 1) * 256, :], f"gwT{bi}")
                        for sub in range(2):
                            e = sdma(sy, sSg[slot], wt_g[2 * bi + sub],
                                     gring[:, slot, :, sub * 128:(sub + 1) * 128])
                        ev[f"wtg_st_{bi}"] = (sSg[slot].sem, e)
                        W(sy, f"cast_u{bi}")
                        stpose(sy, sT, uring[:, slot, :, :],
                               uw_bf[bi * 256:(bi + 1) * 256, :], f"uwT{bi}")
                        for sub in range(2):
                            e = sdma(sy, sSu[slot], wt_u[2 * bi + sub],
                                     uring[:, slot, :, sub * 128:(sub + 1) * 128])
                        ev[f"wtu_st_{bi}"] = (sSu[slot].sem, e)

                def emit_pipe_d_load(ht):
                    slot = ht % 2

                    @block.sync
                    def _(sy):
                        if ht >= 2:
                            W(sy, f"dload_used_{ht - 2}")
                        W(sy, f"cast_d{ht}")
                        sdma(sy, sDl[slot], dstage[:, slot, :],
                             dw_bf[ht * 128:(ht + 1) * 128, :], f"dload_{ht}")

                def emit_pipe_d_store(ht):
                    slot = ht % 2

                    @block.sync
                    def _(sy):
                        W(sy, f"dwT{ht}")
                        sdma(sy, sSd[slot], wt_d[ht], dring[:, slot, :, :],
                             f"wtd_st_{ht}")

                def emit_dw_transpose(ht, tag):
                    slot = ht % 2
                    NBATCH = 6

                    @block.tensor
                    def _(t):
                        W(t, f"dload_{ht}", "p0_small")
                        if ht == 0:
                            W(t, f"{tag}_hdn_{NIT - 1}", f"{tag}_hdn_{NIT - 2}")
                        if ht >= 2:
                            W(t, f"wtd_st_{ht - 2}")
                        for b in range(NBATCH):
                            its = range(8 * b, min(8 * b + 8, NIT))
                            pscr = (pA if b % 2 == 0 else pB)[:, 0:512].bitcast(BF16)
                            if b >= 2:
                                W(t, f"dwTev_{ht}_{b - 2}")
                            elif ht >= 1:
                                W(t, f"dwTev_{ht - 1}_{NBATCH - 2 + b}")
                            for k, it in enumerate(its):
                                inst = t.transpose(
                                    pscr[:, k * 128:(k + 1) * 128],
                                    dstage[:, slot, it * 128:(it + 1) * 128],
                                    id_bf[:, :])
                            nm = cT.mark(inst, f"dwT_{ht}_{b}")
                            if b == NBATCH - 1:
                                ev[f"dload_used_{ht}"] = (cT.sem, nm)

                    for b in range(NBATCH):
                        its = range(8 * b, min(8 * b + 8, NIT))
                        nits = len(its)
                        pscr = (pA if b % 2 == 0 else pB)[:, 0:512].bitcast(BF16)

                        @block.vector
                        def _(v, _b=b, _n=nits, _ps=pscr):
                            W(v, f"dwT_{ht}_{_b}")
                            if ht >= 2 and _b == 0 and FFN_SUB >= 4:
                                W(v, f"{tag}_dn_mm_{ht - 2}")
                            v.tensor_copy(
                                dring[:, slot, 8 * _b:8 * _b + _n, :],
                                _ps[:, 0:_n * 128])
                            ev[f"dwTev_{ht}_{_b}"] = (cV.sem, cV.n)
                            nm = cV.n
                            if _b == NBATCH - 1:
                                ev[f"dwT{ht}"] = (cV.sem, nm)

                prev = [None]

                def ffn_chunk(ci, src_kind):
                    tag = f"{ph}{src_kind}{ci}"
                    direct = with_transposes and prev[0] is None
                    ptag = prev[0]

                    if not direct:
                        @block.gpsimd
                        def _(g):
                            if ptag is not None:
                                W(g, f"{ptag}_stored", f"{ptag}_stored0")
                            if src_kind == "o2" and ptag is None:
                                W(g, *[f"x2st_{tt}" for tt in range(NTT)])
                            for tt in range(4):
                                row = ci * 4 + tt
                                if src_kind == "o1":
                                    src = x_own_in[row * 128:(row + 1) * 128, :]
                                elif src_kind == "m":
                                    src = x_mir_in[row * 128:(row + 1) * 128, :]
                                else:
                                    src = xspill[row]
                                sdma(g, sStg[tt], xstage[:, tt, :], src,
                                     f"{tag}_staged{tt}")

                    for tt in range(4):
                        ev[f"{tag}_ln{tt}_src"] = ev[f"{tag}_staged{tt}"]
                        pre = []
                        if tt >= 2:
                            pre.append(f"{tag}_xnT{tt - 2}")
                        elif ptag is not None:
                            pre.append(f"{ptag}_xnT{tt + 2}")
                        emit_ln(f"{tag}_ln{tt}", xstage[:, tt, :], xn_bf[:, tt % 2, :],
                                cb=(tt % 2) * 4, pre_out_waits=tuple(pre))

                        @block.sync
                        def _(sy, _tt=tt):
                            W(sy, f"{tag}_ln{_tt}_out")
                            if _tt == 0 and ptag is not None:
                                W(sy, f"{ptag}_gu_mm_{NIT - 1}")
                            stpose(sy, sT, xnT_ch[:, :, _tt * 128:(_tt + 1) * 128],
                                   xn_bf[:, _tt % 2, :], f"{tag}_xnT{_tt}")

                    if direct and FFN_SUB >= 2:
                        for bi in range(2, NB_GU):
                            emit_pipe_block(bi)
                        if FFN_SUB >= 3:
                            emit_pipe_d_load(0)
                            emit_pipe_d_load(1)

                    if not direct:
                        @block.sync
                        def _(sy):
                            for bi in range(NB_GU):
                                slot = bi % 2
                                if bi >= 2:
                                    W(sy, f"{tag}_gu_mm_{2 * (bi - 2) + 1}")
                                elif ptag is not None:
                                    W(sy, f"{ptag}_gu_mm_{NIT - 4 + 2 * bi + 1}")
                                W(sy, f"wtg_st_{bi}", f"wtu_st_{bi}")
                                if bi < 2:
                                    W(sy, f"wtg_st_{NB_GU - 2 + bi}",
                                      f"wtu_st_{NB_GU - 2 + bi}")
                                for sub in range(2):
                                    sdma(sy, sGr[slot],
                                         gring[:, slot, :, sub * 128:(sub + 1) * 128],
                                         wt_g[2 * bi + sub])
                                    e = sdma(sy, sUr[slot],
                                             uring[:, slot, :, sub * 128:(sub + 1) * 128],
                                             wt_u[2 * bi + sub])
                                ev[f"{tag}_guw{bi}"] = (sUr[slot].sem, e)
                                ev[f"{tag}_guwg{bi}"] = (sGr[slot].sem, sGr[slot].n)
                            for ht in range(NCT):
                                slot = ht % 2
                                if ht >= 2:
                                    W(sy, f"{tag}_dn_mm_{ht - 2}")
                                elif ptag is not None:
                                    W(sy, f"{ptag}_dn_mm_{NCT - 2 + ht}")
                                W(sy, f"wtd_st_{ht}")
                                if ht < 2:
                                    W(sy, f"wtd_st_{NCT - 2 + ht}")
                                sdma(sy, sDr[slot], dring[:, slot, :, :],
                                     wt_d[ht], f"{tag}_dw{ht}")

                    if FFN_SUB < 2:
                        prev[0] = tag
                        return
                    @block.tensor
                    def _(t):
                        for it in range(NIT):
                            bi, sub = it // 2, it % 2
                            slot = bi % 2
                            bank = pA if it % 2 == 0 else pB
                            if sub == 0:
                                if direct:
                                    W(t, f"gwT{bi}", f"uwT{bi}")
                                else:
                                    W(t, f"{tag}_guw{bi}", f"{tag}_guwg{bi}")
                            if it == 0:
                                for tt in range(4):
                                    W(t, f"{tag}_xnT{tt}")
                            if it >= 2:
                                W(t, f"{tag}_hdn_{it - 2}")
                            elif ptag is not None:
                                W(t, f"{ptag}_hdn_{NIT - 2 + it}")
                            for ct in range(NCT):
                                t.matmul(bank[:, 0:512],
                                         gring[:, slot, ct, sub * 128:(sub + 1) * 128],
                                         xnT_ch[:, ct, :],
                                         start=(ct == 0), stop=(ct == NCT - 1))
                            for ct in range(NCT):
                                inst = t.matmul(
                                    bank[:, 512:1024],
                                    uring[:, slot, ct, sub * 128:(sub + 1) * 128],
                                    xnT_ch[:, ct, :],
                                    start=(ct == 0), stop=(ct == NCT - 1))
                            nm = cT.mark(inst, f"{tag}_gu_mm_{it}")
                            if direct:
                                ev[f"gu_mm_done_{it}"] = (cT.sem, nm)

                    for it in range(NIT):
                        bank = pA if it % 2 == 0 else pB

                        @block.scalar
                        def _(s, _it=it, _bank=bank):
                            W(s, f"{tag}_gu_mm_{_it}")
                            if _it == 0 and direct:
                                W(s, "p0_small")
                            if _it < 2 and ptag is not None:
                                W(s, f"{ptag}_hdn_{NIT - 2 + _it}")
                            s.activation(silu_b[:, _it % 2, :], _bank[:, 0:512],
                                                 AF.Silu, scale=gs_sb[:, _it:_it + 1])
                            ev[f"{tag}_silu_{_it}"] = (cS.sem, cS.n)

                        @block.vector
                        def _(v, _it=it, _bank=bank):
                            W(v, f"{tag}_silu_{_it}")
                            if _it == 0 and ptag is not None:
                                W(v, f"{ptag}_dn_mm_{NCT - 1}")
                            v.scalar_tensor_tensor(
                                hdn[:, _it, :], _bank[:, 512:1024],
                                us_sb[:, _it:_it + 1], silu_b[:, _it % 2, :],
                                ALU.mult, ALU.mult)
                            ev[f"{tag}_hdn_{_it}"] = (cV.sem, cV.n)

                    if FFN_SUB < 4 and not (direct and FFN_SUB >= 3):
                        prev[0] = tag
                        return

                    def emit_dev_dtb_res(ht):
                        hp = pC[:, 0:512] if ht % 2 == 0 else pC[:, 512:1024]
                        pD = pD1 if ht % 2 == 0 else pD2

                        @block.vector
                        def _(v, _ht=ht, _hp=hp):
                            W(v, f"{tag}_dn_mm_{_ht}")
                            if _ht == 0 and direct:
                                W(v, "p0_small")
                            if _ht >= 2 and FFN_SUB >= 5:
                                W(v, f"{tag}_dtb_{_ht - 2}")
                            elif ptag is not None:
                                W(v, f"{ptag}_dtb_{NCT - 2 + _ht}")
                            v.tensor_scalar(dhat[:, _ht % 2, :], _hp,
                                                    ds_sb[:, _ht:_ht + 1], 0.5,
                                                    ALU.mult, ALU.mult)
                            ev[f"{tag}_dev_{_ht}"] = (cV.sem, cV.n)

                        if FFN_SUB < 5:
                            return
                        @block.tensor
                        def _(t, _ht=ht, _pD=pD):
                            W(t, f"{tag}_dev_{_ht}")
                            if _ht == 0 and direct:
                                W(t, "p0_small")
                            if _ht >= 2:
                                W(t, f"{tag}_res_{_ht - 2}")
                            elif ptag is not None:
                                W(t, f"{ptag}_res_{NCT - 2 + _ht}")
                            for sub in range(4):
                                inst = t.transpose(
                                    _pD[:, sub * 256:sub * 256 + 128],
                                    dhat[:, _ht % 2, sub * 128:(sub + 1) * 128],
                                    id_bf[:, :])
                            cT.mark(inst, f"{tag}_dtb_{_ht}")

                        @block.vector
                        def _(v, _ht=ht, _pD=pD):
                            W(v, f"{tag}_dtb_{_ht}")
                            for sub in range(4):
                                inst = v.tensor_tensor(
                                    xstage[:, sub, _ht * 128:(_ht + 1) * 128],
                                    xstage[:, sub, _ht * 128:(_ht + 1) * 128],
                                    _pD[:, sub * 256:sub * 256 + 128], ALU.add)
                            inst
                            ev[f"{tag}_res_{_ht}"] = (cV.sem, cV.n)

                    if direct:
                        for ht in range(NCT):
                            emit_dw_transpose(ht, tag)
                            emit_pipe_d_store(ht)
                            if ht + 2 < NCT:
                                emit_pipe_d_load(ht + 2)
                            if FFN_SUB < 4:
                                continue

                            @block.tensor
                            def _(t, _ht=ht):
                                slot = _ht % 2
                                hp = pC[:, 0:512] if _ht % 2 == 0 else pC[:, 512:1024]
                                W(t, f"dwT{_ht}")
                                if _ht >= 2:
                                    W(t, f"{tag}_dev_{_ht - 2}")
                                for it in range(NIT):
                                    inst = t.matmul(hp, dring[:, slot, it, :],
                                                    hdn[:, it, :],
                                                    start=(it == 0),
                                                    stop=(it == NIT - 1))
                                nm = cT.mark(inst, f"{tag}_dn_mm_{_ht}")
                                ev[f"dn_mm_done0_{_ht}"] = (cT.sem, nm)
                            emit_dev_dtb_res(ht)
                    if FFN_SUB < 4:
                        prev[0] = tag
                        return
                    if not direct:
                        for ht in range(NCT):
                            @block.tensor
                            def _(t, _ht=ht):
                                slot = _ht % 2
                                hp = (pC[:, 0:512] if _ht % 2 == 0
                                      else pC[:, 512:1024])
                                if _ht == 0:
                                    W(t, f"{tag}_hdn_{NIT - 1}",
                                      f"{tag}_hdn_{NIT - 2}")
                                W(t, f"{tag}_dw{_ht}")
                                if _ht >= 2:
                                    W(t, f"{tag}_dev_{_ht - 2}")
                                elif ptag is not None:
                                    W(t, f"{ptag}_dev_{NCT - 2 + _ht}")
                                for it in range(NIT):
                                    inst = t.matmul(hp, dring[:, slot, it, :],
                                                    hdn[:, it, :],
                                                    start=(it == 0),
                                                    stop=(it == NIT - 1))
                                cT.mark(inst, f"{tag}_dn_mm_{_ht}")
                            emit_dev_dtb_res(ht)

                    if FFN_SUB < 5:
                        prev[0] = tag
                        return
                    @block.gpsimd
                    def _(g):
                        W(g, *[f"{tag}_res_{ht}" for ht in range(NCT)])
                        for tt in range(4):
                            row = ci * 4 + tt
                            if src_kind == "o1":
                                sdma(g, sRs[tt % 2], xspill[row], xstage[:, tt, :])
                            elif src_kind == "m":
                                sdma(g, sRs[tt % 2], x1m_scr[row], xstage[:, tt, :])
                            else:
                                sdma(g, sRs[tt % 2],
                                     y_out[row * 128:(row + 1) * 128, :],
                                     xstage[:, tt, :])
                        ev[f"{tag}_stored"] = (sRs[1].sem, sRs[1].n)
                        ev[f"{tag}_stored0"] = (sRs[0].sem, sRs[0].n)

                    prev[0] = tag

                if with_transposes:
                    @block.gpsimd
                    def _(g):
                        for tt in range(4):
                            sdma(g, sStg[tt], xstage[:, tt, :],
                                 x_own_in[tt * 128:(tt + 1) * 128, :],
                                 f"{ph}o10_staged{tt}")
                    emit_bulk_casts()
                    emit_pipe_block(0)
                    emit_pipe_block(1)
                for (ci, kind) in chunks:
                    ffn_chunk(ci, kind)
                return prev[0]

        try:
            if PH_LIMIT < 1:
                raise _PhaseStop()
            last1 = emit_ffn_phase("f1",
                                   [(0, "o1"), (1, "o1"), (0, "m"), (1, "m")][:CH_COUNT],
                                   with_transposes=True)
            barrier("p1", extra=((f"{last1}_stored", f"{last1}_stored0")
                                 if FFN_SUB >= 5 else ()))
            if PH_LIMIT < 2:
                raise _PhaseStop()

            with contextlib.ExitStack() as pR:
                def rsb(name, shape, dt):
                    return pR.enter_context(sbuf_t("R" + name, shape, dt,
                                                           side="right"))

                qT = rsb("qT", [128, NCT, TOWN], BF16)
                encT = rsb("encT", [128, NH, TOWN], BF16)
                r_cols = rsb("rcols", [128, NTT, NH], F32)

                with contextlib.ExitStack() as p2:
                    def psb(name, shape, dt):
                        return p2.enter_context(sbuf_t("p2" + name, shape, dt))

                    ktab = psb("ktab", [128, NKVT, 2, 128], F32)
                    xn2_bf = psb("xn2", [128, 2, D], BF16)
                    xsf = psb("xsf", [128, 2, D], F32)
                    xsb = psb("xsb", [128, 2, D], BF16)
                    xnTw = psb("xnTw", [128, NCT, 256], BF16)
                    wkb = psb("wkb", [128, NCT, KVD], BF16)
                    wvb = psb("wvb", [128, NCT, KVD], BF16)
                    khat = psb("khat", [128, 2, KVD], F32)
                    ropet = psb("ropet", [128, 2, KVD], F32)
                    krot = psb("krot", [128, 2, KVD], BF16)
                    vrot = psb("vrot", [128, 2, KVD], BF16)

                    with contextlib.ExitStack() as skv:
                        pK = skv.enter_context(psum_t("pK", [128, 512], F32))
                        pV = skv.enter_context(psum_t("pVp", [128, 512], F32))
                        pK2 = skv.enter_context(psum_t("pK2", [128, 512], F32))
                        pV2 = skv.enter_context(psum_t("pV2", [128, 512], F32))

                        @block.gpsimd
                        def _(g):
                            sdma(g, sCn, ktab[:, :, :, :], ktab_in, "ktab_ld")
                            for ct in range(NCT):
                                sdma(g, sCa[ct % 4], wkb[:, ct, :],
                                     wk_in[ct * 128:(ct + 1) * 128, :])
                                e = sdma(g, sCa[ct % 4], wvb[:, ct, :],
                                         wv_in[ct * 128:(ct + 1) * 128, :])
                                ev[f"wkvb_{ct}"] = (sCa[ct % 4].sem, e)
                        for kt in range(NKVT):
                            ev[f"ktab{kt}_done"] = ev["ktab_ld"]

                        pair_list = [(8, 9), (10, 11), (12, 13), (14, 15),
                                     (0, 1), (2, 3), (4, 5), (6, 7)]
                        for pi, (lt0, lt1) in enumerate(pair_list):
                            mir = lt0 >= 8
                            ptag = f"kvp{pi}"
                            stage = xsb if mir else xsf

                            @block.gpsimd
                            def _(g, _pi=pi, _lt0=lt0, _mir=mir, _st=stage, _ptag=ptag):
                                if _pi >= 1:
                                    W(g, f"kvp{_pi - 1}_ln1_out")
                                for j in range(2):
                                    row = (_lt0 % 8) + j
                                    srcd = x1m_scr[row] if _mir else xspill[row]
                                    e = sdma(g, cG, _st[:, j, :], srcd)
                                ev[f"{_ptag}_staged"] = (cG.sem, e)

                            for j in range(2):
                                ev[f"{ptag}_ln{j}_src"] = ev[f"{ptag}_staged"]
                                pre = [f"kvp{pi - 1}_xnT{j}"] if pi >= 1 else []
                                emit_ln(f"{ptag}_ln{j}", stage[:, j, :], xn2_bf[:, j, :],
                                        cb=j * 4, pre_out_waits=tuple(pre))

                                @block.sync
                                def _(sy, _j=j, _ptag=ptag, _pi=pi):
                                    W(sy, f"{_ptag}_ln{_j}_out")
                                    if _pi >= 1:
                                        W(sy, f"kvp{_pi - 1}_vmm1")
                                    e = cH.mark(sy.dma_start_transpose(
                                        xnTw[:, :, _j * 128:(_j + 1) * 128],
                                        xn2_bf[:, _j, :]), f"{_ptag}_xnT{_j}")
                                    sy.wait_ge(cH.sem, e)

                            @block.tensor
                            def _(t, _pi=pi, _ptag=ptag):
                                W(t, f"{_ptag}_xnT0", f"{_ptag}_xnT1")
                                if _pi == 0:
                                    W(t, *[f"wkvb_{ct}" for ct in range(NCT)])
                                if _pi >= 1:
                                    W(t, f"kvp{_pi - 1}_khat0", f"kvp{_pi - 1}_khat1",
                                      f"kvp{_pi - 1}_vev0", f"kvp{_pi - 1}_vev1")
                                for ct in range(NCT):
                                    st_, sp_ = (ct == 0), (ct == NCT - 1)
                                    t.matmul(pK[:, :], xnTw[:, ct, 0:128],
                                             wkb[:, ct, :], start=st_, stop=sp_)
                                    t.matmul(pK2[:, :], xnTw[:, ct, 128:256],
                                             wkb[:, ct, :], start=st_, stop=sp_)
                                    t.matmul(pV[:, :], xnTw[:, ct, 0:128],
                                             wvb[:, ct, :], start=st_, stop=sp_)
                                    inst = t.matmul(pV2[:, :], xnTw[:, ct, 128:256],
                                                    wvb[:, ct, :], start=st_, stop=sp_)
                                cT.mark(inst, f"{_ptag}_mm_c{NCT - 1}")
                                ev[f"{_ptag}_vmm1"] = (cT.sem, cT.n)

                            for j, lt in enumerate((lt0, lt1)):
                                tg = f"{ptag}k{j}"
                                pKx = pK if j == 0 else pK2
                                pVx = pV if j == 0 else pV2
                                cb = 8 + j * 4

                                @block.scalar
                                def _(s, _tg=tg, _pK=pKx, _cb=cb, _ptag=ptag):
                                    W(s, f"{_ptag}_mm_c{NCT - 1}")
                                    s.activation(sq_junk[:, 0:KVD], _pK[:, :],
                                                         AF.Square,
                                                         accum_out=lncol[:, _cb:_cb + 1])
                                    ev[f"{_tg}_ssq"] = (cS.sem, cS.n)

                                @block.vector
                                def _(v, _tg=tg, _cb=cb):
                                    W(v, f"{_tg}_ssq")
                                    v.tensor_scalar(lncol[:, _cb + 1:_cb + 2],
                                                            lncol[:, _cb:_cb + 1],
                                                            1.0 / KVD, RMS_EPS,
                                                            ALU.mult, ALU.add)
                                    ev[f"{_tg}_ms"] = (cV.sem, cV.n)

                                @block.scalar
                                def _(s, _tg=tg, _cb=cb):
                                    W(s, f"{_tg}_ms")
                                    s.activation(lncol[:, _cb + 2:_cb + 3],
                                                         lncol[:, _cb + 1:_cb + 2],
                                                         AF.Sqrt)
                                    ev[f"{_tg}_sd"] = (cS.sem, cS.n)

                                @block.vector
                                def _(v, _tg=tg, _cb=cb, _lt=lt, _j=j, _pK=pKx,
                                      _pi=pi, _ptag=ptag):
                                    W(v, f"{_tg}_sd", f"ktab{_lt}_done")
                                    if _pi >= 1:
                                        W(v, f"kvp{_pi - 1}_kst{_j}")
                                    v.reciprocal(lncol[:, _cb + 3:_cb + 4],
                                                 lncol[:, _cb + 2:_cb + 3])
                                    v.tensor_scalar(khat[:, _j, :], _pK[:, :],
                                                    lncol[:, _cb + 3:_cb + 4], None,
                                                    ALU.mult)
                                    kh = khat[:, _j, :].rearrange("p (n h) -> p n h", n=NKV)
                                    rt = ropet[:, _j, :].rearrange("p (n h) -> p n h", n=NKV)
                                    kr = krot[:, _j, :].rearrange("p (n h) -> p n h", n=NKV)
                                    cosb = ktab[:, _lt, 1, :].unsqueeze(1).broadcast_to(
                                        [128, NKV, 128])
                                    sinlo = ktab[:, _lt, 0, 0:64].unsqueeze(1).broadcast_to(
                                        [128, NKV, 64])
                                    sinhi = ktab[:, _lt, 0, 64:128].unsqueeze(1).broadcast_to(
                                        [128, NKV, 64])
                                    v.tensor_tensor(rt, kh, cosb, ALU.mult)
                                    v.tensor_tensor(kr[:, :, 0:64], kh[:, :, 64:128],
                                                    sinlo, ALU.mult)
                                    v.tensor_tensor(kr[:, :, 64:128], kh[:, :, 0:64],
                                                    sinhi, ALU.mult)
                                    v.tensor_tensor(krot[:, _j, :],
                                                            ropet[:, _j, :],
                                                            krot[:, _j, :], ALU.add)
                                    ev[f"{_ptag}_khat{_j}"] = (cV.sem, cV.n)

                                @block.vector
                                def _(v, _tg=tg, _j=j, _pV=pVx, _pi=pi, _ptag=ptag):
                                    W(v, f"{_ptag}_mm_c{NCT - 1}")
                                    if _pi >= 1:
                                        W(v, f"kvp{_pi - 1}_vst{_j}")
                                    v.tensor_copy(vrot[:, _j, :], _pV[:, :])
                                    ev[f"{_ptag}_vev{_j}"] = (cV.sem, cV.n)

                                @block.gpsimd
                                def _(g, _ptag=ptag, _j=j, _lt=lt):
                                    W(g, f"{_ptag}_khat{_j}")
                                    sdma(g, cG, k_scr[_lt], krot[:, _j, :],
                                         f"{_ptag}_kst{_j}")
                                    W(g, f"{_ptag}_vev{_j}")
                                    sdma(g, cG, v_scr[_lt], vrot[:, _j, :],
                                         f"{_ptag}_vst{_j}")

                        def xw_body(rank):
                            m = rank % 2
                            g = nc.gpsimd
                            W(g, "kvp7_kst0", "kvp7_kst1", "kvp7_vst0", "kvp7_vst1")
                            for lt in range(NKVT):
                                sdma(g, sXc[lt % 2], xch_k[m, lt], k_scr[lt])
                                sdma(g, sXc[lt % 2], xch_v[m, lt], v_scr[lt])
                            g.wait_ge(sXc[0].sem, sXc[0].n)
                            g.wait_ge(sXc[1].sem, sXc[1].n)
                            sdma(g, sXf, xch_f[m, :], magic_sb[:, :], "flagw")

                        @block.gpsimd
                        def _(g):
                            core_switch([nc.gpsimd], xw_body)

                    with contextlib.ExitStack() as sq:
                        def qsb(name, shape, dt):
                            return sq.enter_context(sbuf_t("q" + name, shape, dt))

                        qhat = qsb("qhat", [128, D], F32)
                        qtmp = qsb("qtmp", [128, D], F32)
                        qrot = qsb("qrot", [128, 2, D], BF16)
                        wq_ring = qsb("wqr", [128, 2, D], BF16)
                        pQ1 = sq.enter_context(psum_t("pQ1", [128, 2048], F32))
                        pQ2 = sq.enter_context(psum_t("pQ2", [128, 2048], F32))

                        for ps in range(4):
                            @block.gpsimd
                            def _(g, _ps=ps):
                                if _ps == 0:
                                    W(g, "kvp7_ln0_out", "kvp7_ln1_out")
                                else:
                                    W(g, f"qp{_ps - 1}_ln1_out")
                                for j in range(2):
                                    e = sdma(g, cG, xsf[:, j, :],
                                             xspill[2 * _ps + j])
                                ev[f"qp{_ps}_staged"] = (cG.sem, e)

                            for j in range(2):
                                tgl = f"qp{ps}_ln{j}"
                                ev[f"{tgl}_src"] = ev[f"qp{ps}_staged"]
                                if ps == 0:
                                    pre = [f"kvp7_xnT{j}"]
                                else:
                                    pre = [f"qp{ps - 1}_xnT{j}"]
                                emit_ln(tgl, xsf[:, j, :], xn2_bf[:, j, :], cb=j * 4,
                                        pre_out_waits=tuple(pre))

                                @block.sync
                                def _(sy, _j=j, _ps=ps):
                                    W(sy, f"qp{_ps}_ln{_j}_out")
                                    if _ps >= 1:
                                        W(sy, f"q_mm_p{_ps - 1}c{NCT - 1}")
                                    elif _j == 0:
                                        W(sy, "kvp7_vmm1")
                                    e = cH.mark(sy.dma_start_transpose(
                                        xnTw[:, :, _j * 128:(_j + 1) * 128],
                                        xn2_bf[:, _j, :]), f"qp{_ps}_xnT{_j}")
                                    sy.wait_ge(cH.sem, e)

                            @block.gpsimd
                            def _(g, _ps=ps):
                                for ct in range(NCT):
                                    slot = ct % 2
                                    if not (_ps == 0 and ct < 2):
                                        pct = ct - 2 if ct >= 2 else NCT - 2 + ct
                                        ppi = _ps if ct >= 2 else _ps - 1
                                        W(g, f"q_mm_p{ppi}c{pct}")
                                    if _ps == 0:
                                        W(g, f"cast_wq{ct // 2}")
                                    e = sdma(g, cG, wq_ring[:, slot, :],
                                             wq_bf[ct * 128:(ct + 1) * 128, :])
                                    ev[f"wq_p{_ps}c{ct}"] = (cG.sem, e)

                            @block.tensor
                            def _(t, _ps=ps):
                                for ct in range(NCT):
                                    slot = ct % 2
                                    W(t, f"wq_p{_ps}c{ct}")
                                    if ct == 0:
                                        W(t, f"qp{_ps}_xnT0", f"qp{_ps}_xnT1")
                                        if _ps >= 1:
                                            W(t, f"q{2 * _ps - 2}_rope",
                                              f"q{2 * _ps - 1}_rope")
                                    for j in range(2):
                                        pQ = pQ1 if j == 0 else pQ2
                                        for nb in range(4):
                                            inst = t.matmul(
                                                pQ[:, nb * 512:(nb + 1) * 512],
                                                xnTw[:, ct, j * 128:(j + 1) * 128],
                                                wq_ring[:, slot, nb * 512:(nb + 1) * 512],
                                                start=(ct == 0), stop=(ct == NCT - 1))
                                        if j == 1 and ct == NCT - 1:
                                            cT.mark(inst, f"q_mm_p{_ps}c{ct}")
                                            ev[f"q_mm_tt{2 * _ps + 1}"] = (cT.sem, cT.n)
                                        elif j == 1:
                                            cT.mark(inst, f"q_mm_p{_ps}c{ct}")
                                        elif ct == NCT - 1:
                                            cT.mark(inst, f"q_mm_tt{2 * _ps}")

                            for j in range(2):
                                tt = 2 * ps + j
                                pQ = pQ1 if j == 0 else pQ2
                                cb = 16 + j * 4
                                tg = f"q{tt}"

                                @block.scalar
                                def _(s, _tg=tg, _pQ=pQ, _cb=cb, _tt=tt):
                                    W(s, f"q_mm_tt{_tt}")
                                    s.activation(sq_junk[:, :], _pQ[:, :],
                                                         AF.Square,
                                                         accum_out=lncol[:, _cb:_cb + 1])
                                    ev[f"{_tg}_ssq"] = (cS.sem, cS.n)

                                @block.vector
                                def _(v, _tg=tg, _cb=cb):
                                    W(v, f"{_tg}_ssq")
                                    v.tensor_scalar(lncol[:, _cb + 1:_cb + 2],
                                                            lncol[:, _cb:_cb + 1],
                                                            1.0 / D, RMS_EPS,
                                                            ALU.mult, ALU.add)
                                    ev[f"{_tg}_ms"] = (cV.sem, cV.n)

                                @block.scalar
                                def _(s, _tg=tg, _cb=cb):
                                    W(s, f"{_tg}_ms")
                                    s.activation(lncol[:, _cb + 2:_cb + 3],
                                                         lncol[:, _cb + 1:_cb + 2],
                                                         AF.Sqrt)
                                    ev[f"{_tg}_sd"] = (cS.sem, cS.n)

                                @block.vector
                                def _(v, _tg=tg, _cb=cb, _tt=tt, _pQ=pQ, _j=j):
                                    W(v, f"{_tg}_sd", f"ktab{_tt}_done")
                                    if _tt >= 2:
                                        W(v, f"qT_{_tt - 2}")
                                    v.reciprocal(lncol[:, _cb + 3:_cb + 4],
                                                 lncol[:, _cb + 2:_cb + 3])
                                    v.tensor_scalar(lncol[:, _cb + 3:_cb + 4],
                                                    lncol[:, _cb + 3:_cb + 4], QSCALE,
                                                    None, ALU.mult)
                                    v.tensor_scalar(qhat[:, :], _pQ[:, :],
                                                    lncol[:, _cb + 3:_cb + 4], None,
                                                    ALU.mult)
                                    qh = qhat[:, :].rearrange("p (n h) -> p n h", n=NH)
                                    tm = qtmp[:, :].rearrange("p (n h) -> p n h", n=NH)
                                    qr = qrot[:, _j, :].rearrange("p (n h) -> p n h", n=NH)
                                    cosb = ktab[:, _tt, 1, :].unsqueeze(1).broadcast_to(
                                        [128, NH, 128])
                                    sinlo = ktab[:, _tt, 0, 0:64].unsqueeze(1).broadcast_to(
                                        [128, NH, 64])
                                    sinhi = ktab[:, _tt, 0, 64:128].unsqueeze(1).broadcast_to(
                                        [128, NH, 64])
                                    v.tensor_tensor(tm, qh, cosb, ALU.mult)
                                    v.tensor_tensor(qr[:, :, 0:64], qh[:, :, 64:128],
                                                    sinlo, ALU.mult)
                                    v.tensor_tensor(qr[:, :, 64:128], qh[:, :, 0:64],
                                                    sinhi, ALU.mult)
                                    v.tensor_tensor(qrot[:, _j, :], qtmp[:, :],
                                                            qrot[:, _j, :], ALU.add)
                                    ev[f"{_tg}_rope"] = (cV.sem, cV.n)
                                    ev[f"q{_tt}_rope"] = ev[f"{_tg}_rope"]

                                @block.sync
                                def _(sy, _tg=tg, _tt=tt, _j=j):
                                    W(sy, f"{_tg}_rope")
                                    e = cH.mark(sy.dma_start_transpose(
                                        qT[:, :, _tt * 128:(_tt + 1) * 128],
                                        qrot[:, _j, :]), f"qT_{_tt}")
                                    sy.wait_ge(cH.sem, e)

                barrier("p2")
                if PH_LIMIT < 3:
                    raise _PhaseStop()

                with contextlib.ExitStack() as p4s:
                    def asb(name, shape, dt):
                        return p4s.enter_context(sbuf_t("a" + name, shape, dt))

                    kT_sb = asb("kT", [128, NKV, SEQ], BF16)
                    v_full = asb("vf", [128, NST, KVD], BF16)

                    P3REGS = []

                    def p3_body(rank):
                        m = rank % 2
                        q = rank % 4
                        qm = (q + 2) % 4
                        q2 = (rank ^ 1) % 4
                        qm2 = (q2 + 2) % 4
                        sy = nc.sync
                        W(sy, "flagw")
                        r_go, r_c, r_f, r_t, r_w = P3REGS
                        sy.reg_mov(r_go, 1)
                        sy.reg_mov(r_c, 0)
                        sy.reg_mov(r_w, 0)
                        with sy.While(r_go):
                            sy.dma_start(out=flag_sb[:, :],
                                         in_=xch_f[1 - m, :]).then_inc(sPoll, 16)
                            sy.reg_alu(r_w, r_w, 16, ALU.add)
                            sy.wait_ge(sPoll, r_w)
                            sy.load(r_f, flag_sb[0:1, 0:1].bitcast(I32))
                            sy.reg_alu(r_f, r_f, MAGIC_BITS, ALU.bitwise_xor)
                            sy.reg_alu(r_c, r_c, 1, ALU.add)
                            sy.reg_alu(r_t, r_c, POLL_BOUND, ALU.is_lt)
                            sy.reg_alu(r_go, r_f, r_t, ALU.logical_and)
                        for gt in range(NST):
                            quad = gt // 8
                            if quad == q:
                                src_k, src_v = k_scr[gt % 8], v_scr[gt % 8]
                            elif quad == qm:
                                src_k, src_v = k_scr[8 + gt % 8], v_scr[8 + gt % 8]
                            elif quad == q2:
                                src_k, src_v = xch_k[1 - m, gt % 8], xch_v[1 - m, gt % 8]
                            else:
                                src_k = xch_k[1 - m, 8 + gt % 8]
                                src_v = xch_v[1 - m, 8 + gt % 8]
                            e = cH.mark(sy.dma_start_transpose(
                                kT_sb[:, :, gt * 128:(gt + 1) * 128], src_k))
                            sy.wait_ge(cH.sem, e)
                            sdma(sy, cH, v_full[:, gt, :], src_v)
                        sy.wait_ge(cH.sem, cH.n)

                    @block.sync
                    def _(sy):
                        with sy.register("r_go") as r_go, sy.register("r_c") as r_c, \
                             sy.register("r_f") as r_f, sy.register("r_t") as r_t, \
                             sy.register("r_w") as r_w:
                            P3REGS.extend([r_go, r_c, r_f, r_t, r_w])
                            core_switch([nc.sync], p3_body)
                        sy.wait_ge(cGc.sem, cGc.n)
                        cGc.mark(sy.nop(), "p3_done")
                    if PH_LIMIT < 4:
                        raise _PhaseStop()

                    with contextlib.ExitStack() as p4c:
                        pT_sb = p4c.enter_context(
                            sbuf_t("pT", [128, 2, 1024], BF16))
                        S_row = p4c.enter_context(sbuf_t("Srow", [1, TOWN], F32))
                        r_row = p4c.enter_context(sbuf_t("rrow", [1, TOWN], F32))
                        pLA = p4c.enter_context(psum_t("pLA", [128, 1024], F32))
                        pLB = p4c.enter_context(psum_t("pLB", [128, 1024], F32))
                        pE = p4c.enter_context(psum_t("pE", [128, 1024], F32))
                        pS = p4c.enter_context(psum_t("pS", [128, 1024], F32))

                        for n in range(NH):
                            kv = n // 4
                            tgh = f"h{n}"

                            @block.tensor
                            def _(t, _n=n, _kv=kv, _tg=tgh):
                                if _n == 0:
                                    W(t, "p3_done", "p0_gc")
                                    for tt2 in range(NTT):
                                        W(t, f"qT_{tt2}")
                                for stp in range(NST + 2):
                                    if stp < NST:
                                        bank = pLA if stp % 2 == 0 else pLB
                                        if stp >= 2:
                                            W(t, f"{_tg}_exp_{stp - 2}")
                                        elif _n > 0:
                                            W(t, f"h{_n - 1}_exp_{NST - 2 + stp}")
                                        for nb in range(2):
                                            inst = t.matmul(
                                                bank[:, nb * 512:(nb + 1) * 512],
                                                kT_sb[:, _kv, stp * 128:(stp + 1) * 128],
                                                qT[:, _n, nb * 512:(nb + 1) * 512],
                                                start=True, stop=True)
                                        cT.mark(inst, f"{_tg}_qk_{stp}")
                                    su = stp - 2
                                    if su >= 0:
                                        W(t, f"{_tg}_exp_{su}")
                                        if su == 0 and _n > 0:
                                            W(t, f"h{_n - 1}_eev", f"h{_n - 1}_rcol")
                                        for nb in range(2):
                                            t.matmul(pE[:, nb * 512:(nb + 1) * 512],
                                                     v_full[:, su, _kv * 128:(_kv + 1) * 128],
                                                     pT_sb[:, su % 2, nb * 512:(nb + 1) * 512],
                                                     start=(su == 0), stop=(su == NST - 1))
                                        for nb in range(2):
                                            inst = t.matmul(
                                                pS[0:1, nb * 512:(nb + 1) * 512],
                                                ones_bf[:, :],
                                                pT_sb[:, su % 2, nb * 512:(nb + 1) * 512],
                                                start=(su == 0), stop=(su == NST - 1))
                                        cT.mark(inst, f"{_tg}_pv_{su}")

                            for stp in range(NST):
                                @block.scalar
                                def _(s, _n=n, _stp=stp, _tg=tgh):
                                    W(s, f"{_tg}_qk_{_stp}")
                                    if _stp >= 2:
                                        W(s, f"{_tg}_pv_{_stp - 2}")
                                    elif _n > 0:
                                        W(s, f"h{_n - 1}_pv_{NST - 2 + _stp}")
                                    bank = pLA if _stp % 2 == 0 else pLB
                                    s.activation(pT_sb[:, _stp % 2, :], bank[:, :],
                                                         AF.Exp, bias=cbias[:, 2:3])
                                    ev[f"{_tg}_exp_{_stp}"] = (cS.sem, cS.n)

                            @block.vector
                            def _(v, _n=n, _tg=tgh):
                                W(v, f"{_tg}_pv_{NST - 1}")
                                v.tensor_copy(encT[:, _n, :], pE[:, :])
                                ev[f"{_tg}_eev"] = (cV.sem, cV.n)
                                v.tensor_copy(S_row[:, :], pS[0:1, :])
                                v.reciprocal(r_row[:, :], S_row[:, :])
                                ev[f"{_tg}_sev"] = (cV.sem, cV.n)

                            @block.tensor
                            def _(t, _n=n, _tg=tgh):
                                W(t, f"{_tg}_eev", f"{_tg}_sev", "p0_idf")
                                for qt in range(NTT):
                                    inst = t.transpose(pE[:, qt:qt + 1],
                                                       r_row[0:1, qt * 128:(qt + 1) * 128],
                                                       id_f[0:1, 0:1])
                                cT.mark(inst, f"{_tg}_rT")

                            @block.vector
                            def _(v, _n=n, _tg=tgh):
                                W(v, f"{_tg}_rT")
                                v.tensor_copy(r_cols[:, :, _n], pE[:, 0:NTT])
                                ev[f"{_tg}_rcol"] = (cV.sem, cV.n)

                        barrier("p4")
                        if PH_LIMIT < 5:
                            raise _PhaseStop()

                    with contextlib.ExitStack() as p5s:
                        enc_tok = p5s.enter_context(
                            sbuf_t("etok", [128, NTT, D], BF16))
                        wo_ring = p5s.enter_context(
                            sbuf_t("wor", [128, 2, D], BF16))
                        x15 = p5s.enter_context(
                            sbuf_t("x15", [128, 2, D], F32))
                        pW1 = p5s.enter_context(psum_t("pW1", [128, 2048], F32))
                        pW2 = p5s.enter_context(psum_t("pW2", [128, 2048], F32))

                        for n in range(NH):
                            @block.sync
                            def _(sy, _n=n):
                                W(sy, f"h{_n}_eev")
                                e = cH.mark(sy.dma_start_transpose(
                                    enc_tok[:, :, _n * 128:(_n + 1) * 128],
                                    encT[:, _n, :]), f"etok_{_n}")
                                sy.wait_ge(cH.sem, e)

                        for qt in range(NTT):
                            @block.vector
                            def _(v, _qt=qt):
                                W(v, *[f"etok_{n}" for n in range(NH)])
                                W(v, *[f"h{n}_rcol" for n in range(NH)])
                                for n in range(NH):
                                    inst = v.tensor_scalar(
                                        enc_tok[:, _qt, n * 128:(n + 1) * 128],
                                        enc_tok[:, _qt, n * 128:(n + 1) * 128],
                                        r_cols[:, _qt, n:n + 1], None, ALU.mult)
                                inst
                                ev[f"enorm_{_qt}"] = (cV.sem, cV.n)

                        encTn = encT
                        for qt in range(NTT):
                            @block.sync
                            def _(sy, _qt=qt):
                                W(sy, f"enorm_{_qt}")
                                if _qt == 0:
                                    W(sy, *[f"etok_{n}" for n in range(NH)])
                                e = cH.mark(sy.dma_start_transpose(
                                    encTn[:, :, _qt * 128:(_qt + 1) * 128],
                                    enc_tok[:, _qt, :]), f"encTn_{_qt}")
                                sy.wait_ge(cH.sem, e)

                        @block.gpsimd
                        def _(g):
                            for tt in range(2):
                                e = sdma(g, cG, x15[:, tt % 2, :], xspill[tt])
                                ev[f"x1r_{tt}"] = (cG.sem, e)

                        def emit_wo_loads(ps):
                            @block.gpsimd
                            def _(g):
                                for ct in range(NCT):
                                    slot = ct % 2
                                    if not (ps == 0 and ct < 2):
                                        pct = ct - 2 if ct >= 2 else NCT - 2 + ct
                                        pps = ps if ct >= 2 else ps - 1
                                        W(g, f"wo_mm_p{pps}c{pct}")
                                    if ps == 0:
                                        W(g, f"cast_wo{ct // 2}")
                                    e = sdma(g, cG, wo_ring[:, slot, :],
                                             wo_bf[ct * 128:(ct + 1) * 128, :])
                                    ev[f"wo_p{ps}c{ct}"] = (cG.sem, e)

                        emit_wo_loads(0)
                        emit_wo_loads(1)

                        @block.tensor
                        def _(t):
                            for ps in range(4):
                                for ct in range(NCT):
                                    slot = ct % 2
                                    W(t, f"wo_p{ps}c{ct}")
                                    if ct == 0:
                                        for qt in range(NTT):
                                            W(t, f"encTn_{qt}")
                                        if ps >= 1:
                                            W(t, f"attnres_{2 * ps - 2}",
                                              f"attnres_{2 * ps - 1}")
                                    for j in range(2):
                                        tt = 2 * ps + j
                                        pW = pW1 if j == 0 else pW2
                                        for nb in range(4):
                                            inst = t.matmul(
                                                pW[:, nb * 512:(nb + 1) * 512],
                                                encTn[:, ct, tt * 128:(tt + 1) * 128],
                                                wo_ring[:, slot, nb * 512:(nb + 1) * 512],
                                                start=(ct == 0), stop=(ct == NCT - 1))
                                        if j == 1 and ct == NCT - 1:
                                            cT.mark(inst, f"wo_mm_p{ps}c{ct}")
                                            ev[f"wo_mm_tt{2 * ps + 1}"] = (cT.sem, cT.n)
                                        elif j == 1:
                                            cT.mark(inst, f"wo_mm_p{ps}c{ct}")
                                        elif ct == NCT - 1:
                                            cT.mark(inst, f"wo_mm_tt{2 * ps}")

                        def emit_p5_store(tt):
                            pW = pW1 if tt % 2 == 0 else pW2

                            @block.vector
                            def _(v, _tt=tt, _pW=pW):
                                W(v, f"wo_mm_tt{_tt}", f"x1r_{_tt}")
                                v.tensor_tensor(x15[:, _tt % 2, :],
                                                        x15[:, _tt % 2, :],
                                                        _pW[:, :], ALU.add)
                                ev[f"attnres_{_tt}"] = (cV.sem, cV.n)

                            @block.gpsimd
                            def _(g, _tt=tt):
                                W(g, f"attnres_{_tt}")
                                sdma(g, sX2[_tt % 2], xspill[_tt],
                                     x15[:, _tt % 2, :], f"x2st_{_tt}")
                                if _tt + 2 < NTT:
                                    W(g, f"x2st_{_tt}")
                                    sdma(g, sX15[_tt % 2], x15[:, _tt % 2, :],
                                         xspill[_tt + 2], f"x1r_{_tt + 2}")

                        emit_p5_store(0)
                        emit_p5_store(1)
                        emit_wo_loads(2)
                        emit_p5_store(2)
                        emit_p5_store(3)
                        emit_wo_loads(3)
                        for tt in range(4, NTT):
                            emit_p5_store(tt)

            barrier("p5", extra=tuple(f"x2st_{tt}" for tt in range(NTT)))
            if PH_LIMIT < 6:
                raise _PhaseStop()

            emit_ffn_phase("f2", [(0, "o2"), (1, "o2")], with_transposes=False)

        except _PhaseStop:
            pass

        @block.sync
        def _(sy):
            sy.wait_ge(cG.sem, cG.n)
            sy.wait_ge(cH.sem, cH.n)
            sy.wait_ge(cT.sem, cT.n)
            sy.wait_ge(cV.sem, cV.n)
            sy.wait_ge(cS.sem, cS.n)
            sy.wait_ge(cGc.sem, cGc.n)

    return nc, ev


def _events_table(ev):
    return {k: v for k, v in ev.items()}


def _build():
    nc1, ev1 = build_program({})
    nc2, ev2 = build_program(ev1)
    assert set(ev1) == set(ev2), (set(ev1) ^ set(ev2))
    return nc2


_NC_CACHE = [None]


def _mk_ktab(pos):
    inv_ts = 10000.0 ** (-np.arange(64, dtype=np.float64) / 64.0)
    ang = pos.astype(np.float64)[:, None] * inv_ts[None, :]
    sn, cs = np.sin(ang), np.cos(ang)
    tab = np.empty((NKVT, 128, 2, 128), np.float64)
    tab[:, :, 0, 0:64] = -sn.reshape(NKVT, 128, 64)
    tab[:, :, 0, 64:128] = sn.reshape(NKVT, 128, 64)
    tab[:, :, 1, 0:64] = cs.reshape(NKVT, 128, 64)
    tab[:, :, 1, 64:128] = cs.reshape(NKVT, 128, 64)
    return np.ascontiguousarray(
        tab.transpose(1, 0, 2, 3).reshape(128, NKVT * 2 * 128)).astype(np.float32)


def kernel(**inputs):
    x = np.asarray(inputs["L0_attn_x"])
    gate_s = np.asarray(inputs["L0_ffn_gate_s"])
    up_s = np.asarray(inputs["L0_ffn_up_s"])
    down_s = np.asarray(inputs["L0_ffn_down_s"])

    if _NC_CACHE[0] is None:
        _NC_CACHE[0] = _build()
    nc = _NC_CACHE[0]

    gs_col = np.ascontiguousarray(gate_s.reshape(NIT, 128).T).astype(np.float32)
    us_col = np.ascontiguousarray(up_s.reshape(NIT, 128).T).astype(np.float32)
    ds_col = np.ascontiguousarray(down_s.reshape(NCT, 128).T).astype(np.float32)

    shared = {
        "wq": np.ascontiguousarray(inputs["L0_attn_Wq"], np.float32),
        "wk": np.ascontiguousarray(inputs["L0_attn_Wk"], np.float32),
        "wv": np.ascontiguousarray(inputs["L0_attn_Wv"], np.float32),
        "wo": np.ascontiguousarray(inputs["L0_attn_Wout"], np.float32),
        "gw": np.ascontiguousarray(inputs["L0_ffn_gate_w"], np.float32),
        "uw": np.ascontiguousarray(inputs["L0_ffn_up_w"], np.float32),
        "dw": np.ascontiguousarray(inputs["L0_ffn_down_w"], np.float32),
        "gs_col": gs_col, "us_col": us_col, "ds_col": ds_col,
        "ident_bf": np.eye(128, dtype=ml_dtypes.bfloat16),
        "ident_f32": np.eye(128, dtype=np.float32),
    }
    in_maps = []
    for r in range(8):
        b, q = r // 4, r % 4
        qm = (q + 2) % 4
        pos = np.concatenate([np.arange(q * TOWN, (q + 1) * TOWN),
                              np.arange(qm * TOWN, (qm + 1) * TOWN)])
        m = dict(shared)
        m["x_own"] = np.ascontiguousarray(x[b, q * TOWN:(q + 1) * TOWN, :], np.float32)
        m["x_mir"] = np.ascontiguousarray(x[b, qm * TOWN:(qm + 1) * TOWN, :], np.float32)
        m["ktab"] = _mk_ktab(pos)
        in_maps.append(m)

    res = run_bass_kernel_spmd(nc, in_maps, core_ids=list(range(8)))
    out = np.empty((2, SEQ, D), np.float32)
    for r in range(8):
        b, q = r // 4, r % 4
        out[b, q * TOWN:(q + 1) * TOWN, :] = res.results[r]["y"]
    return out
